# version 17
# baseline (speedup 1.0000x reference)
import contextlib

import numpy as np
import ml_dtypes

import concourse.bass as bass
import concourse.mybir as mybir
from concourse.bass_utils import run_bass_kernel_spmd

F32 = mybir.dt.float32
BF16 = mybir.dt.bfloat16
I32 = mybir.dt.int32
I16 = mybir.dt.int16
AF = mybir.ActivationFunctionType
ALU = mybir.AluOpType
AX = mybir.AxisListType

B = 2
D = 2048
E = 8704
EXPAND = 4096
INTER = 5504
DCH = D // 128
ICH = INTER // 128
EPS = 1e-5
SCALE_QK = 0.0625

BF = ml_dtypes.bfloat16


class Builder:
    def __init__(self, SB):
        self.SB = SB
        self.nc = bass.Bass("TRN2", target_bir_lowering=False)
        self.st = contextlib.ExitStack()
        self.gen = 0
        self.bar_cnt = 0
        self.bar = self.st.enter_context(self.nc.semaphore("bar"))
        self.sems = {}
        self.waited = {e: {} for e in "STVAG"}
        self.ops = []
        self.sem_serial = 0
        self.uid_n = 0
        self.new_phase_sems()

    def uid(self, n):
        self.uid_n += 1
        return "%s_u%d" % (n, self.uid_n)

    def new_phase_sems(self):
        self.sem_serial += 1
        self.sems = {}
        for k in ("T", "V", "A", "G"):
            h = self.st.enter_context(
                self.nc.semaphore("s%d%s" % (self.sem_serial, k)))
            self.sems[k] = [h, 0]

    def _qsem(self, k):
        if k not in self.sems:
            self.sem_serial += 1
            h = self.st.enter_context(
                self.nc.semaphore("s%d%s" % (self.sem_serial,
                                             k.replace(":", "_"))))
            self.sems[k] = [h, 0]
        return self.sems[k]

    def last_tok(self, k):
        ent = self.sems[k]
        return (ent[0], ent[1], self.gen)

    def _tok(self, k, amt):
        ent = self._qsem(k) if ":" in k else self.sems[k]
        if ent[1] + amt > 48000:
            self.sem_serial += 1
            h = self.st.enter_context(
                self.nc.semaphore("s%d%s" % (self.sem_serial, k)))
            self.sems[k] = ent = [h, 0]
        ent[1] += amt
        return (ent[0], ent[1], self.gen)

    def _emit_waits(self, eng, waits):
        best = {}
        for t in waits:
            if t is None:
                continue
            h, val, g = t
            if g < self.gen:
                continue
            if best.get(h, -1) < val:
                best[h] = val
        use = []
        wd = self.waited[eng]
        for h, val in best.items():
            if wd.get(h, -1) < val:
                wd[h] = val
                use.append((h, val))
        return use

    def op(self, eng, fn, waits=(), inc=False, sync_after=False):
        use = self._emit_waits(eng, waits)
        tok = self._tok(eng, 1) if (inc or sync_after) else None
        fence = tok if sync_after else None
        if fence is not None:
            self.waited[eng][fence[0]] = fence[1]

        def run(e, fn=fn, use=tuple(use), tok=tok, fence=fence):
            for h, val in use:
                e.wait_ge(h, val)
            ins = fn(e)
            if tok is not None:
                ins.then_inc(tok[0], 1)
            if fence is not None:
                e.wait_ge(fence[0], fence[1])

        self.ops.append((eng, run))
        return tok

    def dma(self, q, dst, src, waits=(), stream="snk"):
        use = self._emit_waits(q, waits)
        tok = self._tok("q" + q + ":" + stream, 16)

        def run(e, dst=dst, src=src, use=tuple(use), tok=tok):
            for h, val in use:
                e.wait_ge(h, val)
            e.dma_start(dst, src).then_inc(tok[0], 16)

        self.ops.append((q, run))
        return tok

    def barrier(self):
        for q in ("S", "G"):
            pref = "q" + q + ":"
            for k, ent in list(self.sems.items()):
                if not k.startswith(pref):
                    continue
                if ent[1] > self.waited[q].get(ent[0], -1):
                    self.waited[q][ent[0]] = ent[1]
                    self.ops.append(
                        (q, lambda e, h=ent[0], v=ent[1]: e.wait_ge(h, v)))
        self.bar_cnt += 5
        tgt = self.bar_cnt
        for eng in "STVAG":
            def run(e, tgt=tgt):
                e.sem_inc(self.bar, 1)
                e.wait_ge(self.bar, tgt)
            self.ops.append((eng, run))
        self.gen += 1
        self.new_phase_sems()

    def flush(self):
        with self.nc.Block() as blk:
            for ek, meth in (("S", blk.sync), ("T", blk.tensor),
                             ("V", blk.vector), ("A", blk.scalar),
                             ("G", blk.gpsimd)):
                items = [f for (k, f) in self.ops if k == ek]
                if items:
                    meth(lambda e, items=items: [f(e) for f in items])
        self.ops = []


def build_nc(SB):
    bd = Builder(SB)
    nc = bd.nc
    OWN = SB // 4
    NSLOT = SB // 512
    NSUP = SB // 1024
    NSUP_OWN = OWN // 1024
    op, dma = bd.op, bd.dma

    ap = lambda *a, **k: nc.dram_tensor(*a, **k).ap()
    xb = ap("xb", [SB, D], F32, kind="ExternalInput")
    xo = ap("xo", [OWN, D], F32, kind="ExternalInput")
    xoT = ap("xoT", [D, OWN], F32, kind="ExternalInput")
    eT = ap("eT", [D, E], F32, kind="ExternalInput")
    pT = ap("pT", [EXPAND, D], F32, kind="ExternalInput")
    gidxT = ap("gidxT", [D, INTER], BF16, kind="ExternalInput")
    uidxT = ap("uidxT", [D, INTER], BF16, kind="ExternalInput")
    didxT = ap("didxT", [INTER, D], BF16, kind="ExternalInput")
    lut16 = ap("lut16", [1, 16], F32, kind="ExternalInput")
    posm = ap("posm", [1, 1], F32, kind="ExternalInput")
    gam = ap("gam", [128, DCH], F32, kind="ExternalInput")
    bet = ap("bet", [128, DCH], F32, kind="ExternalInput")
    ones1 = ap("ones1", [1, 128], F32, kind="ExternalInput")
    idf = ap("idf", [128, 128], F32, kind="ExternalInput")
    idb = ap("idb", [128, 128], BF16, kind="ExternalInput")
    ramp = ap("ramp", [128, 512], F32, kind="ExternalInput")
    keep = ap("keep", [128, 512], F32, kind="ExternalInput")
    dtab = ap("dtab", [128, 16], F32, kind="ExternalInput")
    csc = ap("csc", [128, 4], F32, kind="ExternalInput")
    jneg = ap("jneg", [128, 16], F32, kind="ExternalInput")
    y = ap("y", [OWN, D], F32, kind="ExternalOutput")

    eTh = ap("eTh", [D, E], BF16)
    eTl = ap("eTl", [D, E], BF16)
    pTh = ap("pTh", [EXPAND, D], BF16)
    pTl = ap("pTl", [EXPAND, D], BF16)
    wgh = ap("wgh", [D, INTER], BF16)
    wgl = ap("wgl", [D, INTER], BF16)
    wuh = ap("wuh", [D, INTER], BF16)
    wul = ap("wul", [D, INTER], BF16)
    wdh = ap("wdh", [INTER, D], BF16)
    wdl = ap("wdl", [INTER, D], BF16)
    kh = ap("kh", [256, SB], BF16)
    kl = ap("kl", [256, SB], BF16)
    vh = ap("vh", [SB, D], BF16)
    vl = ap("vl", [SB, D], BF16)
    glh = ap("glh", [D, OWN], BF16)
    gll = ap("gll", [D, OWN], BF16)
    afh = ap("afh", [D, OWN], BF16)
    afl = ap("afl", [D, OWN], BF16)
    r0s = ap("r0s", [D, OWN], F32)
    hsh = ap("hsh", [INTER, OWN], BF16)
    hsl = ap("hsl", [INTER, OWN], BF16)

    xoT3 = xoT.rearrange("(c p) t -> p c t", p=128)
    r0s3 = r0s.rearrange("(c p) t -> p c t", p=128)

    sbufg = lambda n, s, d: bd.st.enter_context(nc.sbuf_tensor(n, s, d))
    c_ones = sbufg("c_ones", [1, 128], F32)
    c_idf = sbufg("c_idf", [128, 128], F32)
    c_idb = sbufg("c_idb", [128, 128], BF16)
    c_csc = sbufg("c_csc", [128, 4], F32)
    c_jneg = sbufg("c_jneg", [128, 16], F32)
    c_gam = sbufg("c_gam", [128, DCH], F32)
    c_bet = sbufg("c_bet", [128, DCH], F32)
    c_lut = sbufg("c_lut", [1, 16], F32)
    c_posm = sbufg("c_posm", [1, 1], F32)
    luthb = sbufg("luthb", [128, 16], F32)
    lutlb = sbufg("lutlb", [128, 16], F32)
    lutbs = sbufg("lutbs", [128, 16], BF16)
    msc = sbufg("msc", [128, 1], F32)
    mR = sbufg("mR", [128, 512], F32)
    tdiag = sbufg("tdiag", [128, 512], F32)
    mdt = sbufg("mdt", [128, 16], F32)
    smt = sbufg("smt", [128, 8], F32)
    cb_idx = sbufg("cb_idx", [128, 1024], BF16)
    cb_acc = sbufg("cb_acc", [128, 1024], F32)
    cb_msk = [sbufg("cb_msk%d" % i, [128, 1024], BF16) for i in range(3)]
    lutfg = sbufg("lutfg", [128, 16], F32)
    cb_hi = sbufg("cb_hi", [128, 1024], BF16)
    cb_lo = sbufg("cb_lo", [128, 1024], BF16)

    cjobs = []
    for c0 in range(0, INTER, 1024):
        w = min(1024, INTER - c0)
        for (src, dh, dl) in ((gidxT, wgh, wgl), (uidxT, wuh, wul)):
            for rr in range(0, D, 128):
                cjobs.append((src, rr, c0, w, dh, dl))
    for c0 in range(0, D, 1024):
        for rr in range(0, INTER, 128):
            cjobs.append((didxT, rr, c0, 1024, wdh, wdl))
    cstate = {"i": 0, "j": 0, "busy": None}
    assert NSUP_OWN >= 1, "SB must be >= 4096"
    _HS = NSLOT // 2
    _visits = _HS * _HS + _HS * (_HS + 1)
    _p3cap = (4 * _visits) // 16
    _rem = len(cjobs) - _p3cap - 57
    CQ2 = 24
    CQ1 = max(0, -(-max(0, _rem - CQ2 * NSUP_OWN) // NSUP))

    def construct_step(nsteps):
        for _ in range(nsteps):
            i = cstate["i"]
            if i >= len(cjobs):
                return
            j = cstate["j"]
            src_, rr, c0, w, dh, dl = cjobs[i]
            h1 = w // 2

            def mk(jj, lo, hi, waits):
                return op("V", lambda e, jj=jj, lo=lo, hi=hi: e.tensor_scalar(
                    cb_msk[jj % 3][:, lo:hi], cb_idx[:, lo:hi],
                    c_jneg[:, jj:jj + 1], None, ALU.is_equal),
                    waits=waits, inc=True)

            if j == 0:
                ld = dma("G", cb_idx[:, 0:w], src_[rr:rr + 128, c0:c0 + w],
                         waits=[cstate["busy"], cstate.get("mlast")],
                         stream="cidx")
                sh = cstate.get("s_hist") or {0: None, 1: None, 2: None}
                mtokA = {}
                mtokB = {}
                for jj in range(3):
                    mtokA[jj] = mk(jj, 0, h1,
                                   [ld, sh[jj]] if jj == 0 else [sh[jj]])
                    mtokB[jj] = mk(jj, h1, w, [sh[jj]])
                cstate["mtokA"] = mtokA
                cstate["mtokB"] = mtokB
                cstate["s_hist"] = sh
                cstate["sA"] = None
                cstate["sB"] = None
            mtokA = cstate["mtokA"]
            mtokB = cstate["mtokB"]
            sh = cstate["s_hist"]
            mb = cb_msk[j % 3]
            if j == 0:
                sA = op("V", lambda e, h1=h1, mb=mb: e.tensor_scalar(
                    cb_acc[:, 0:h1], mb[:, 0:h1], lutfg[:, 0:1], None,
                    ALU.mult), waits=[mtokA[0], cstate.get("hi_prev")],
                    inc=True)
                sB = op("V", lambda e, h1=h1, w=w, mb=mb: e.tensor_scalar(
                    cb_acc[:, h1:w], mb[:, h1:w], lutfg[:, 0:1], None,
                    ALU.mult), waits=[mtokB[0]], inc=True)
            else:
                sA = op("V", lambda e, h1=h1, j=j, mb=mb:
                        e.scalar_tensor_tensor(
                            cb_acc[:, 0:h1], mb[:, 0:h1],
                            lutfg[:, j:j + 1], cb_acc[:, 0:h1],
                            ALU.mult, ALU.add),
                        waits=[mtokA[j], cstate["sA"]], inc=True)
                sB = op("V", lambda e, h1=h1, w=w, j=j, mb=mb:
                        e.scalar_tensor_tensor(
                            cb_acc[:, h1:w], mb[:, h1:w],
                            lutfg[:, j:j + 1], cb_acc[:, h1:w],
                            ALU.mult, ALU.add),
                        waits=[mtokB[j], cstate["sB"]], inc=True)
            if j + 3 < 16:
                mtokA[j + 3] = mk(j + 3, 0, h1, [sA])
                mtokB[j + 3] = mk(j + 3, h1, w, [sB])
            sh[j % 3] = sB
            cstate["sA"] = sA
            cstate["sB"] = sB
            if j < 15:
                cstate["j"] = j + 1
                continue
            cstate["mlast"] = mtokB[15]
            hi = op("A", lambda e, w=w: e.copy(cb_hi[:, 0:w], cb_acc[:, 0:w]),
                    waits=[sB, cstate["busy"]], sync_after=True)
            lo = op("V", lambda e, w=w: e.tensor_tensor(
                cb_lo[:, 0:w], cb_acc[:, 0:w], cb_hi[:, 0:w], ALU.subtract),
                waits=[hi, cstate["busy"]], inc=True)
            cstate["hi_prev"] = hi
            dma("G", dh[rr:rr + 128, c0:c0 + w], cb_hi[:, 0:w],
                waits=[lo], stream="cwb")
            cstate["busy"] = dma("G", dl[rr:rr + 128, c0:c0 + w],
                                 cb_lo[:, 0:w], stream="cwb")
            cstate["i"] = i + 1
            cstate["j"] = 0

    def construct_quantum(njobs):
        construct_step(16 * njobs)

    with contextlib.ExitStack() as ph:
        psb = lambda n, s, d: ph.enter_context(nc.sbuf_tensor(n, s, d))
        pp = ph.enter_context(nc.psum_tensor("pp0", [128, 512], F32))
        c_ramp = psb("c_ramp", [128, 512], F32)
        c_keep = psb("c_keep", [128, 512], F32)
        c_dtab = psb("c_dtab", [128, 16], F32)
        ldtok = None
        for dst, src in ((c_ones, ones1), (c_idf, idf), (c_idb, idb),
                         (c_ramp, ramp), (c_keep, keep), (c_dtab, dtab),
                         (c_csc, csc), (c_jneg, jneg), (c_gam, gam),
                         (c_bet, bet),
                         (c_lut, lut16), (c_posm, posm)):
            ldtok = dma("S", dst[:], src, stream="c")
        op("T", lambda e: e.matmul(pp[:, 0:16], c_ones[:], c_lut[:],
                                   start=True, stop=True), waits=[ldtok])
        ttok = op("T", lambda e: e.matmul(pp[:, 16:17], c_ones[:], c_posm[:],
                                          start=True, stop=True), inc=True)
        lutf = lutfg
        pm = psb("pm", [128, 1], F32)
        op("V", lambda e: e.tensor_copy(lutf[:], pp[:, 0:16]), waits=[ttok])
        op("V", lambda e: e.tensor_copy(pm[:], pp[:, 16:17]), sync_after=True)
        op("V", lambda e: e.tensor_copy(lutbs[:], lutf[:]), sync_after=True)
        op("V", lambda e: e.tensor_copy(luthb[:], lutbs[:]), sync_after=True)
        op("V", lambda e: e.tensor_tensor(lutlb[:], lutf[:], luthb[:],
                                          ALU.subtract), sync_after=True)
        op("V", lambda e: e.tensor_copy(lutbs[:], lutlb[:]), sync_after=True)
        vt = op("V", lambda e: e.tensor_copy(lutlb[:], lutbs[:]),
                sync_after=True)
        op("A", lambda e: e.activation(smt[:, 0:1], pm[:], AF.Abs),
           waits=[vt], sync_after=True)
        at = op("A", lambda e: e.activation(smt[:, 1:2], smt[:, 0:1], AF.Exp,
                                            scale=-1.0), sync_after=True)
        vt = op("V", lambda e: e.tensor_scalar(smt[:, 2:3], smt[:, 1:2], 1.0,
                                               None, ALU.add), waits=[at],
                sync_after=True)
        op("A", lambda e: e.activation(smt[:, 3:4], smt[:, 2:3], AF.Ln),
           waits=[vt], sync_after=True)
        at = op("A", lambda e: e.activation(smt[:, 4:5], pm[:], AF.Relu),
                sync_after=True)
        op("V", lambda e: e.tensor_tensor(msc[:], smt[:, 3:4], smt[:, 4:5],
                                          ALU.add), waits=[at], sync_after=True)
        op("V", lambda e: e.tensor_scalar(mR[:], c_ramp[:], msc[:], None,
                                          ALU.mult))
        op("V", lambda e: e.tensor_scalar(smt[:, 5:6], c_dtab[:, 0:1], msc[:],
                                          None, ALU.mult), sync_after=True)
        op("V", lambda e: e.tensor_scalar(tdiag[:], mR[:], smt[:, 5:6], None,
                                          ALU.add), sync_after=True)
        op("V", lambda e: e.tensor_tensor(tdiag[:], tdiag[:], c_keep[:],
                                          ALU.mult), sync_after=True)
        tneg = psb("tneg", [128, 512], F32)
        op("V", lambda e: e.tensor_scalar(tneg[:], c_keep[:], 1e30, -1e30,
                                          ALU.mult, ALU.add), sync_after=True)
        op("V", lambda e: e.tensor_tensor(tdiag[:], tdiag[:], tneg[:],
                                          ALU.add))
        op("V", lambda e: e.tensor_scalar(mdt[:], c_dtab[:], msc[:], None,
                                          ALU.mult), sync_after=True)

        spl_f = [psb("spl_f%d" % i, [128, E], F32) for i in range(2)]
        spl_h = [psb("spl_h%d" % i, [128, E], BF16) for i in range(2)]
        spl_l = [psb("spl_l%d" % i, [128, E], BF16) for i in range(2)]
        wb_tok = [None, None]

        def split_rows(srcF, dstH, dstL, nrow, ncol):
            for rr in range(0, nrow, 128):
                b = (rr // 128) % 2
                ld = dma("S", spl_f[b][:, 0:ncol], srcF[rr:rr + 128, :],
                         waits=[wb_tok[b]], stream="sp%d" % b)
                op("V", lambda e, b=b, n=ncol: e.tensor_copy(
                    spl_h[b][:, 0:n], spl_f[b][:, 0:n]), waits=[ld],
                    sync_after=True)
                vtk = op("V", lambda e, b=b, n=ncol: e.tensor_tensor(
                    spl_l[b][:, 0:n], spl_f[b][:, 0:n], spl_h[b][:, 0:n],
                    ALU.subtract), sync_after=True)
                dma("G", dstH[rr:rr + 128, :], spl_h[b][:, 0:ncol],
                    waits=[vtk], stream="spw%d" % b)
                wb_tok[b] = dma("G", dstL[rr:rr + 128, :], spl_l[b][:, 0:ncol],
                                stream="spw%d" % b)

        split_rows(eT, eTh, eTl, D, E)
        split_rows(pT, pTh, pTl, EXPAND, D)
        bd.barrier()
        bd.flush()

    def ln_tm(src, dst, sst, extra_wait=()):
        op("A", lambda e: e.activation(dst[:], src[:], AF.Identity,
                                       accum_out=sst[:, 0:1]),
           waits=list(extra_wait), sync_after=True)
        at = op("A", lambda e: e.activation(dst[:], src[:], AF.Square,
                                            accum_out=sst[:, 1:2]),
                sync_after=True)
        op("V", lambda e, sst=sst: e.tensor_scalar(
            sst[:, 2:3], sst[:, 0:1], -1.0 / D, None, ALU.mult), waits=[at],
            sync_after=True)
        op("V", lambda e, sst=sst: e.scalar_tensor_tensor(
            sst[:, 3:4], sst[:, 2:3], 1.0, sst[:, 2:3], ALU.mult, ALU.mult))
        op("V", lambda e, sst=sst: e.tensor_scalar(
            sst[:, 4:5], sst[:, 1:2], 1.0 / D, None, ALU.mult),
            sync_after=True)
        vt = op("V", lambda e, sst=sst: e.tensor_tensor(
            sst[:, 4:5], sst[:, 4:5], sst[:, 3:4], ALU.subtract),
            sync_after=True)
        at = op("A", lambda e, sst=sst: e.activation(
            sst[:, 5:6], sst[:, 4:5], AF.Sqrt, bias=c_csc[:, 0:1]),
            waits=[vt], sync_after=True)
        op("V", lambda e, sst=sst: e.reciprocal(sst[:, 6:7], sst[:, 5:6]),
           waits=[at], sync_after=True)
        return op("V", lambda e, sst=sst: e.tensor_scalar(
            dst[:], src[:], sst[:, 2:3], sst[:, 6:7], ALU.add, ALU.mult),
            sync_after=True)

    def expand_phase(x_src, nsup, is_own, cq):
        lb = 512 if is_own else 2560
        pb = 4608 if is_own else 6656
        qb = 0 if is_own else 256
        with contextlib.ExitStack() as ph:
            psb = lambda n, s, d: ph.enter_context(
                nc.sbuf_tensor(bd.uid(n), s, d))
            pf = ph.enter_context(nc.psum_tensor(bd.uid("pf"), [128, 3072], F32))
            pbf = ph.enter_context(nc.psum_tensor(bd.uid("pbf"), [128, 1024], BF16))
            xbuf = [psb("xb%d" % i, [128, D], F32) for i in range(2)]
            x0 = psb("x0", [128, D], F32)
            ht = psb("ht", [128, D], F32)
            hfp = psb("hfp", [128, D], F32)
            h_hi = psb("h_hi", [128, DCH * 1024], BF16)
            h_lo = psb("h_lo", [128, DCH * 1024], BF16)
            nwb = 1 if is_own else 2
            wb_h = [psb("wbh%d" % i, [128, DCH * 512], BF16)
                    for i in range(nwb)]
            wb_l = [psb("wbl%d" % i, [128, DCH * 512], BF16)
                    for i in range(nwb)]
            gbuf = psb("gbuf", [128, 512], F32)
            vf32 = psb("vf32", [128, 512], F32)
            e_hi = psb("e_hi", [128, 512], BF16)
            e_lo = psb("e_lo", [128, 512], BF16)
            rstg = psb("rstg", [128, 2048], F32) if is_own else None
            vstg_h = None if is_own else psb("vstg_h", [128, 2 * 4 * 256], BF16)
            vstg_l = None if is_own else psb("vstg_l", [128, 2 * 4 * 256], BF16)
            sst = psb("sst", [128, 8], F32)

            h_hi3 = h_hi.ap().rearrange("p (c t) -> p c t", c=DCH)
            h_lo3 = h_lo.ap().rearrange("p (c t) -> p c t", c=DCH)
            hfpv = hfp.ap().rearrange("p (c t) -> p c t", c=DCH)
            rstgv = (rstg.ap().rearrange("p (c t) -> p c t", c=DCH)
                     if is_own else None)

            ln_free = None
            wb_free = [None, None]

            bank_free = [None] * 4
            pbf_free = None
            e_free = None
            gb_free = None
            vstg_free = None
            rstg_free = None

            xb_free = [None, None]
            vt = None
            for s in range(nsup):
                h_tok = None
                for i in range(8):
                    t0 = 1024 * s + 128 * i
                    b = i % 2
                    vt = ln_tm(xbuf[b], x0, sst, extra_wait=[
                        dma("S", xbuf[b][:], x_src[t0:t0 + 128, :],
                            waits=[xb_free[b]], stream="x%d" % b), vt])
                    xb_free[b] = vt
                    vt = ln_tm(x0, ht, sst, extra_wait=[vt])
                    for rnd in range(2):
                        tt = None
                        for cc in range(8):
                            c = 8 * rnd + cc
                            tt = op("T", lambda e, c=c, cc=cc: e.matmul(
                                pf[:, 2048 + 128 * cc:2048 + 128 * (cc + 1)],
                                ht[:, 128 * c:128 * (c + 1)], c_idf[:],
                                is_transpose=True, start=(cc % 4 == 0),
                                stop=(cc % 4 == 3)),
                                waits=[vt, ln_free] if cc == 0 else [],
                                inc=(cc == 7))
                        for cc in range(8):
                            c = 8 * rnd + cc
                            ln_free = op("A", lambda e, c=c, cc=cc: e.activation(
                                hfp[:, 128 * c:128 * (c + 1)],
                                pf[:, 2048 + 128 * cc:2048 + 128 * (cc + 1)],
                                AF.Identity, bias=c_bet[:, c:c + 1],
                                scale=c_gam[:, c:c + 1]), waits=[tt],
                                inc=(cc == 7))
                    at_h = op("A", lambda e, i=i: e.copy(
                        h_hi3[:, :, 128 * i:128 * (i + 1)], hfpv),
                        waits=[ln_free], sync_after=True)
                    h_tok = op("V", lambda e, i=i: e.tensor_tensor(
                        h_lo3[:, :, 128 * i:128 * (i + 1)], hfpv,
                        h_hi3[:, :, 128 * i:128 * (i + 1)], ALU.subtract),
                        waits=[at_h], inc=True)
                    if is_own:
                        ldx = dma("S", rstgv[:, :, :],
                                  xoT3[:, :, t0:t0 + 128], waits=[rstg_free],
                                  stream="rx")
                        vt2 = None
                        for rnd in range(2):
                            tt = None
                            for cc in range(8):
                                c = 8 * rnd + cc
                                tt = op("T", lambda e, c=c, cc=cc: e.matmul(
                                    pf[:, 2048 + 128 * cc:
                                       2048 + 128 * (cc + 1)],
                                    x0[:, 128 * c:128 * (c + 1)], c_idf[:],
                                    is_transpose=True, start=(cc % 4 == 0),
                                    stop=(cc % 4 == 3)),
                                    waits=[h_tok, ln_free] if cc == 0 else [],
                                    inc=(cc == 7))
                            for cc in range(8):
                                c = 8 * rnd + cc
                                vt2 = op("V", lambda e, c=c, cc=cc:
                                         e.scalar_tensor_tensor(
                                             rstgv[:, c, :],
                                             pf[:, 2048 + 128 * cc:
                                                2048 + 128 * (cc + 1)], 1.0,
                                             rstgv[:, c, :], ALU.mult, ALU.add),
                                         waits=[tt, ldx], inc=(cc == 7))
                            ln_free = vt2
                        vt = vt2
                        rstg_free = dma(
                            "G", r0s3[:, :, t0:t0 + 128], rstgv[:, :, :],
                            waits=[vt2], stream="rstg")

                slabs = [[("kq", 0), ("kq", 1)]]
                for j0 in range(0, 16, 2):
                    slabs.append([("lin", j0), ("lin", j0 + 1),
                                  ("pg", j0), ("pg", j0 + 1)])

                for si, slab in enumerate(slabs):
                    par = (s * len(slabs) + si) % nwb
                    wbh_c = wb_h[par]
                    wbl_c = wb_l[par]
                    lt = None
                    for dc in range(DCH):
                        for ei0 in range(0, len(slab), 2):
                            kind, j = slab[ei0]
                            ecol = (qb + 128 * j if kind == "kq" else
                                    (lb if kind == "lin" else pb) + 128 * j)
                            lt = dma("S",
                                     wbh_c[:, (dc * 4 + ei0) * 128:
                                           (dc * 4 + ei0 + 2) * 128],
                                     eTh[128 * dc:128 * (dc + 1),
                                         ecol:ecol + 256],
                                     waits=[wb_free[par]],
                                     stream="wb%d" % par)
                            lt = dma("S",
                                     wbl_c[:, (dc * 4 + ei0) * 128:
                                           (dc * 4 + ei0 + 2) * 128],
                                     eTl[128 * dc:128 * (dc + 1),
                                         ecol:ecol + 256],
                                     stream="wb%d" % par)
                    for tb in range(2):
                        mm_tok = [None] * len(slab)
                        for ei in range(len(slab)):
                            po = pf[:, 512 * ei:512 * (ei + 1)]
                            first = True
                            for dc in range(DCH):
                                lh = wbh_c[:, (dc * 4 + ei) * 128:
                                           (dc * 4 + ei + 1) * 128]
                                llo = wbl_c[:, (dc * 4 + ei) * 128:
                                            (dc * 4 + ei + 1) * 128]
                                rh = h_hi3[:, dc, 512 * tb:512 * (tb + 1)]
                                rl = h_lo3[:, dc, 512 * tb:512 * (tb + 1)]
                                w0 = [lt, h_tok, bank_free[ei]] if first else []
                                op("T", lambda e, po=po, lh=lh, rh=rh, f=first:
                                   e.matmul(po, lh, rh, start=f, stop=False),
                                   waits=w0)
                                first = False
                                op("T", lambda e, po=po, lh=lh, rl=rl:
                                   e.matmul(po, lh, rl, start=False,
                                            stop=False))
                                mm_tok[ei] = op(
                                    "T", lambda e, po=po, llo=llo, rh=rh,
                                    l=(dc == DCH - 1): e.matmul(
                                        po, llo, rh, start=False, stop=l),
                                    inc=(dc == DCH - 1))
                        tg = 1024 * s + 512 * tb
                        if slab[0][0] == "kq":
                            for ei in range(2):
                                qc = slab[ei][1]
                                at_e = op("A", lambda e, ei=ei: e.copy(
                                    e_hi[:], pf[:, 512 * ei:512 * (ei + 1)]),
                                    waits=[mm_tok[ei], e_free],
                                    sync_after=True)
                                vt = op("V", lambda e, ei=ei: e.tensor_tensor(
                                    e_lo[:], pf[:, 512 * ei:512 * (ei + 1)],
                                    e_hi[:], ALU.subtract), waits=[at_e],
                                    sync_after=True)
                                if is_own:
                                    op("V", lambda e, qc=qc, tg=tg:
                                       e.tensor_copy(
                                           q_hi[:, qc * OWN + tg:
                                                qc * OWN + tg + 512], e_hi[:]))
                                    vt = op("V", lambda e, qc=qc, tg=tg:
                                            e.tensor_copy(
                                                q_lo[:, qc * OWN + tg:
                                                     qc * OWN + tg + 512],
                                                e_lo[:]), inc=True)
                                    bank_free[ei] = vt
                                    e_free = vt
                                else:
                                    dma("G", kh[128 * qc:128 * (qc + 1),
                                                tg:tg + 512], e_hi[:],
                                        waits=[vt], stream="eout")
                                    e_free = dma(
                                        "G", kl[128 * qc:128 * (qc + 1),
                                                tg:tg + 512], e_lo[:],
                                        stream="eout")
                                    bank_free[ei] = vt
                        else:
                            for pi in range(2):
                                j = slab[pi][1]
                                plin = pf[:, 512 * pi:512 * (pi + 1)]
                                ppg = pf[:, 512 * (2 + pi):512 * (3 + pi)]
                                at = op("A", lambda e, ppg=ppg: e.activation(
                                    gbuf[:], ppg, AF.Gelu),
                                    waits=[mm_tok[2 + pi], gb_free], inc=True)
                                gb_free = op(
                                    "V", lambda e, plin=plin: e.tensor_tensor(
                                        vf32[:], plin, gbuf[:], ALU.mult),
                                    waits=[at, mm_tok[pi]], sync_after=True)
                                at_e = op("A", lambda e: e.copy(e_hi[:],
                                                                 vf32[:]),
                                          waits=[e_free, gb_free],
                                          sync_after=True)
                                vt = op("V", lambda e: e.tensor_tensor(
                                    e_lo[:], vf32[:], e_hi[:], ALU.subtract),
                                    waits=[at_e], sync_after=True)
                                bank_free[pi] = vt
                                bank_free[2 + pi] = vt
                                if is_own:
                                    dma("G", glh[128 * j:128 * (j + 1),
                                                 tg:tg + 512], e_hi[:],
                                        waits=[vt], stream="eout")
                                    e_free = dma(
                                        "G", gll[128 * j:128 * (j + 1),
                                                 tg:tg + 512], e_lo[:],
                                        stream="eout")
                                else:
                                    tt = None
                                    for half, srct in ((0, e_hi), (1, e_lo)):
                                        for ks in range(4):
                                            sl = 4 * half + ks
                                            tt = op("T", lambda e, srct=srct,
                                                    ks=ks, sl=sl: e.matmul(
                                                        pbf[:, 128 * sl:
                                                            128 * (sl + 1)],
                                                        srct[:, 128 * ks:
                                                             128 * (ks + 1)],
                                                        c_idb[:],
                                                        is_transpose=True,
                                                        start=(sl == 0),
                                                        stop=(sl == 7)),
                                                    waits=[vt, pbf_free]
                                                    if sl == 0 else [],
                                                    inc=(sl == 7))
                                    e_free = tt
                                    jj = j % 2
                                    for ks in range(4):
                                        cb = (tb * 4 + ks) * 256 + 128 * jj
                                        op("V", lambda e, ks=ks, cb=cb:
                                           e.tensor_copy(
                                               vstg_h[:, cb:cb + 128],
                                               pbf[:, 128 * ks:
                                                   128 * (ks + 1)]),
                                           waits=[tt, vstg_free]
                                           if ks == 0 else [tt])
                                        pbf_free = op(
                                            "V", lambda e, ks=ks, cb=cb:
                                            e.tensor_copy(
                                                vstg_l[:, cb:cb + 128],
                                                pbf[:, 128 * (4 + ks):
                                                    128 * (5 + ks)]),
                                            inc=(ks == 3))
                                    if jj == 1:
                                        dvc = 128 * (j - 1)
                                        for ks in range(4):
                                            trow = tg + 128 * ks
                                            cb = (tb * 4 + ks) * 256
                                            dma("G", vh[trow:trow + 128,
                                                        dvc:dvc + 256],
                                                vstg_h[:, cb:cb + 256],
                                                waits=[pbf_free],
                                                stream="vflush")
                                            vstg_free = dma(
                                                "G", vl[trow:trow + 128,
                                                        dvc:dvc + 256],
                                                vstg_l[:, cb:cb + 256],
                                                stream="vflush")
                        st = si * 2 + tb
                        construct_step((16 * cq) // 18 +
                                       (1 if st < (16 * cq) % 18 else 0))
                    wb_free[par] = mm_tok[len(slab) - 1]
            bd.barrier()
            bd.flush()

    expand_phase(xb, NSUP, False, CQ1)

    with contextlib.ExitStack() as qst:
        q_hi = qst.enter_context(nc.sbuf_tensor("q_hi", [128, 2 * OWN], BF16))
        q_lo = qst.enter_context(nc.sbuf_tensor("q_lo", [128, 2 * OWN], BF16))

        expand_phase(xo, NSUP_OWN, True, CQ2)

        with contextlib.ExitStack() as ph:
            psb = lambda n, s, d: ph.enter_context(
                nc.sbuf_tensor(bd.uid(n), s, d))
            psc = ph.enter_context(nc.psum_tensor("psc", [128, 1024], F32))
            pav = ph.enter_context(nc.psum_tensor("pav", [128, 2048], F32))
            ppt = [ph.enter_context(nc.psum_tensor("ppt%d" % i,
                                                   [128, 1024], BF16))
                   for i in range(2)]
            HS = NSLOT // 2
            acc = [psb("acc%d" % i, [128, 2048], F32) for i in range(HS)]
            vbuf_h = [psb("vbh%d" % i, [128, 4 * 2048], BF16) for i in range(2)]
            vbuf_l = [psb("vbl%d" % i, [128, 4 * 2048], BF16) for i in range(2)]
            kbuf_h = [psb("kbh%d" % i, [128, 2 * 512], BF16) for i in range(2)]
            kbuf_l = [psb("kbl%d" % i, [128, 2 * 512], BF16) for i in range(2)]
            s2 = [psb("s2%d" % i, [128, 512], F32) for i in range(2)]
            ppf = [psb("ppf%d" % i, [128, 512], F32) for i in range(2)]
            p_hi = [psb("p_hi%d" % i, [128, 512], BF16) for i in range(2)]
            p_lo = [psb("p_lo%d" % i, [128, 512], BF16) for i in range(2)]
            pt_hi = [psb("pt_hi%d" % i, [128, 512], BF16) for i in range(2)]
            pt_lo = [psb("pt_lo%d" % i, [128, 512], BF16) for i in range(2)]
            fst = psb("fst", [128, 6 * HS], F32)
            atm = psb("atm", [128, 1024], F32)
            a_hi = psb("a_hi", [128, 1024], BF16)
            a_lo = psb("a_lo", [128, 1024], BF16)
            astg = psb("astg", [128, 512], BF16)

            vld_tok = [None, None]
            fin_ppt = [0]
            fin_vt = [None]
            pav_free = None
            ppt_free = [None, None]
            psc_free = [None, None]
            astg_free = None
            s2_free = [None, None]
            af_free = None
            vcnt = [0]
            vprev = [None]
            pavf = [None, None]
            fsum_t = [None]

            def visit_back(ctx):
                sbk = ctx['sbk']
                mi = ctx['mi']
                Fc = ctx['Fc']
                first = ctx['first']
                at = ctx['at']
                vtp = ctx['vtp']
                phv, plv = ctx['phv'], ctx['plv']
                pthv, ptlv = ctx['pthv'], ctx['ptlv']
                pptv = ctx['pptv']
                tt = None
                for half, srct in ((0, phv), (1, plv)):
                    for ks in range(4):
                        sl = 4 * half + ks
                        tt = op("T", lambda e, srct=srct, ks=ks,
                                sl=sl, pptv=pptv: e.matmul(
                                    pptv[:, 128 * sl:128 * (sl + 1)],
                                    srct[:, 128 * ks:128 * (ks + 1)],
                                    c_idb[:], is_transpose=True,
                                    start=(sl == 0), stop=(sl == 7)),
                                waits=[vtp, ppt_free[sbk]]
                                if sl == 0 else [], inc=(sl == 7))
                op("A", lambda e, pthv=pthv, pptv=pptv:
                   e.copy(pthv[:], pptv[:, 0:512]), waits=[tt])
                ppt_free[sbk] = op(
                    "A", lambda e, ptlv=ptlv, pptv=pptv: e.copy(
                        ptlv[:], pptv[:, 512:1024]), inc=True)
                for hf in range(2):
                    avtok = None
                    for dv4 in (2 * hf, 2 * hf + 1):
                        pA = pav[:, 512 * dv4:512 * (dv4 + 1)]
                        fmm = True
                        for ks in range(4):
                            vh_s = vbuf_h[ctx['b']][:, 2048 * ks + 512 * dv4:
                                             2048 * ks + 512 * (dv4 + 1)]
                            vl_s = vbuf_l[ctx['b']][:, 2048 * ks + 512 * dv4:
                                             2048 * ks + 512 * (dv4 + 1)]
                            ph_s = pthv[:, 128 * ks:128 * (ks + 1)]
                            pl_s = ptlv[:, 128 * ks:128 * (ks + 1)]
                            w0 = ([ppt_free[sbk], pavf[hf]]
                                  if (fmm and dv4 == 2 * hf) else [])
                            op("T", lambda e, pA=pA, ph_s=ph_s, vh_s=vh_s,
                               f=fmm: e.matmul(pA, ph_s, vh_s, start=f,
                                               stop=False), waits=w0)
                            fmm = False
                            op("T", lambda e, pA=pA, ph_s=ph_s, vl_s=vl_s:
                               e.matmul(pA, ph_s, vl_s, start=False,
                                        stop=False))
                            avtok = op("T", lambda e, pA=pA, pl_s=pl_s,
                                       vh_s=vh_s, l=(ks == 3): e.matmul(
                                           pA, pl_s, vh_s, start=False,
                                           stop=l),
                                       inc=(ks == 3 and dv4 == 2 * hf + 1))
                    hs0 = 1024 * hf
                    if first:
                        pavf[hf] = op("V", lambda e, mi=mi, hs0=hs0:
                                      e.tensor_copy(
                                          acc[mi][:, hs0:hs0 + 1024],
                                          pav[:, hs0:hs0 + 1024]),
                                      waits=[avtok, at], inc=True)
                    else:
                        pavf[hf] = op("V", lambda e, mi=mi, Fc=Fc, hs0=hs0:
                                      e.scalar_tensor_tensor(
                                          acc[mi][:, hs0:hs0 + 1024],
                                          acc[mi][:, hs0:hs0 + 1024],
                                          Fc[:, 5:6],
                                          pav[:, hs0:hs0 + 1024],
                                          ALU.mult, ALU.add),
                                      waits=[avtok, at], inc=True)
                if first:
                    fsum = op("V", lambda e, Fc=Fc: e.tensor_copy(
                        Fc[:, 2:3], Fc[:, 3:4]), inc=True)
                else:
                    fsum = op("V", lambda e, Fc=Fc:
                              e.scalar_tensor_tensor(
                                  Fc[:, 2:3], Fc[:, 2:3],
                                  Fc[:, 5:6], Fc[:, 3:4],
                                  ALU.mult, ALU.add), inc=True)
                fsum_t[0] = fsum
                if ctx['last']:
                    vld_tok[ctx['b']] = pavf[1]

            for sweep in range(2):
                m_lo, m_hi = sweep * HS, (sweep + 1) * HS
                for c in range(m_hi):
                    b = c % 2
                    lt = None
                    for ks in range(4):
                        r0 = 512 * c + 128 * ks
                        lt = dma("S", vbuf_h[b][:, 2048 * ks:2048 * (ks + 1)],
                                 vh[r0:r0 + 128, :], waits=[vld_tok[b]],
                                 stream="vk%d" % b)
                        lt = dma("S", vbuf_l[b][:, 2048 * ks:2048 * (ks + 1)],
                                 vl[r0:r0 + 128, :], stream="vk%d" % b)
                    for qc in range(2):
                        lt = dma("S", kbuf_h[b][:, 512 * qc:512 * (qc + 1)],
                                 kh[128 * qc:128 * (qc + 1),
                                    512 * c:512 * (c + 1)], stream="vk%d" % b)
                        lt = dma("S", kbuf_l[b][:, 512 * qc:512 * (qc + 1)],
                                 kl[128 * qc:128 * (qc + 1),
                                    512 * c:512 * (c + 1)], stream="vk%d" % b)
                    for m in range(max(c, m_lo), m_hi):
                        mi = m - m_lo
                        if vprev[0] is not None and vprev[0]['mi'] == mi:
                            visit_back(vprev[0])
                            vprev[0] = None
                        first = (c == 0)
                        diag = (c == m)
                        sbk = vcnt[0] % 2
                        vcnt[0] += 1
                        pS = psc[:, 512 * sbk:512 * (sbk + 1)]
                        s2v = s2[sbk]
                        ppfv = ppf[sbk]
                        phv, plv = p_hi[sbk], p_lo[sbk]
                        pthv, ptlv = pt_hi[sbk], pt_lo[sbk]
                        pptv = ppt[sbk]
                        so = 4 * sbk
                        stok = None
                        for qc in range(2):
                            qh_s = q_hi[:, qc * OWN + 128 * m:
                                        qc * OWN + 128 * (m + 1)]
                            ql_s = q_lo[:, qc * OWN + 128 * m:
                                        qc * OWN + 128 * (m + 1)]
                            kh_s = kbuf_h[b][:, 512 * qc:512 * (qc + 1)]
                            kl_s = kbuf_l[b][:, 512 * qc:512 * (qc + 1)]
                            op("T", lambda e, pS=pS, qh_s=qh_s, kh_s=kh_s,
                               f=(qc == 0): e.matmul(pS, qh_s, kh_s, start=f,
                                                     stop=False),
                               waits=[lt, psc_free[sbk]] if qc == 0 else [])
                            op("T", lambda e, pS=pS, qh_s=qh_s, kl_s=kl_s:
                               e.matmul(pS, qh_s, kl_s, start=False,
                                        stop=False))
                            stok = op("T", lambda e, pS=pS, ql_s=ql_s,
                                      kh_s=kh_s, l=(qc == 1): e.matmul(
                                          pS, ql_s, kh_s, start=False, stop=l),
                                      inc=(qc == 1))
                        mask = tdiag if diag else mR
                        psc_free[sbk] = op(
                            "V", lambda e, pS=pS, mask=mask, s2v=s2v:
                            e.scalar_tensor_tensor(s2v[:], pS, SCALE_QK,
                                                   mask[:], ALU.mult, ALU.add),
                            waits=[stok, s2_free[sbk]], sync_after=True)
                        op("V", lambda e, s2v=s2v, so=so: e.tensor_reduce(
                            smt[:, so:so + 1], s2v[:], AX.X, ALU.max,
                            negate=True), sync_after=True)
                        if not diag:
                            kk = m - c
                            op("V", lambda e, kk=kk, so=so: e.tensor_tensor(
                                smt[:, so:so + 1], smt[:, so:so + 1],
                                mdt[:, kk:kk + 1], ALU.subtract),
                               sync_after=True)
                        Fc = fst[:, 6 * mi:6 * mi + 6]
                        if first:
                            op("V", lambda e, Fc=Fc, so=so: e.tensor_copy(
                                Fc[:, 0:1], smt[:, so:so + 1]),
                               sync_after=True)
                        else:
                            op("V", lambda e, Fc=Fc: e.tensor_copy(
                                Fc[:, 1:2], Fc[:, 0:1]), sync_after=True)
                            op("V", lambda e, Fc=Fc, so=so: e.tensor_tensor(
                                Fc[:, 0:1], Fc[:, 1:2], smt[:, so:so + 1],
                                ALU.min), sync_after=True)
                            op("V", lambda e, Fc=Fc: e.tensor_tensor(
                                Fc[:, 4:5], Fc[:, 0:1], Fc[:, 1:2],
                                ALU.subtract), sync_after=True)
                        if diag:
                            vt = op("V", lambda e, Fc=Fc, so=so: e.tensor_copy(
                                smt[:, so + 1:so + 2], Fc[:, 0:1]), inc=True)
                        else:
                            kk = m - c
                            vt = op("V", lambda e, Fc=Fc, kk=kk, so=so:
                                    e.tensor_tensor(smt[:, so + 1:so + 2],
                                                    Fc[:, 0:1],
                                                    mdt[:, kk:kk + 1],
                                                    ALU.add), inc=True)
                        s2_free[sbk] = op(
                            "A", lambda e, Fc=Fc, ppfv=ppfv, s2v=s2v, so=so:
                            e.activation(
                                ppfv[:], s2v[:], AF.Exp,
                                bias=smt[:, so + 1:so + 2],
                                accum_out=Fc[:, 3:4]), waits=[vt],
                            sync_after=True)
                        at = s2_free[sbk]
                        if not first:
                            at = op("A", lambda e, Fc=Fc: e.activation(
                                Fc[:, 5:6], Fc[:, 4:5], AF.Exp),
                                sync_after=True)
                        at_p = op("A", lambda e, phv=phv, ppfv=ppfv:
                                  e.copy(phv[:], ppfv[:]),
                                  waits=[at], sync_after=True)
                        vtp = op("V",
                                 lambda e, plv=plv, ppfv=ppfv, phv=phv:
                                 e.tensor_tensor(
                                     plv[:], ppfv[:], phv[:], ALU.subtract),
                                 waits=[at_p], inc=True)
                        ctx = dict(sbk=sbk, b=b, mi=mi, Fc=Fc,
                                   first=first, at=at, vtp=vtp,
                                   phv=phv, plv=plv, pthv=pthv,
                                   ptlv=ptlv, pptv=pptv,
                                   last=(m == m_hi - 1))
                        if vprev[0] is not None:
                            visit_back(vprev[0])
                        vprev[0] = ctx
                        construct_step(4)


                if vprev[0] is not None:
                    visit_back(vprev[0])
                    vprev[0] = None
                for mi in range(HS):
                    m = m_lo + mi
                    Fc = fst[:, 6 * mi:6 * mi + 6]
                    op("V", lambda e, Fc=Fc: e.reciprocal(
                        smt[:, 2:3], Fc[:, 2:3]),
                       waits=[pavf[0], pavf[1], fsum_t[0]],
                       sync_after=True)
                    for ch in range(2):
                        vt_n = op("V", lambda e, mi=mi, ch=ch:
                                  e.tensor_scalar(
                                      atm[:], acc[mi][:, 1024 * ch:
                                                      1024 * (ch + 1)],
                                      smt[:, 2:3], None, ALU.mult),
                                  waits=[fin_vt[0]], sync_after=True)
                        at_a = op("A", lambda e: e.copy(a_hi[:], atm[:]),
                                  waits=[af_free, vt_n], sync_after=True)
                        vt = op("V", lambda e: e.tensor_tensor(
                            a_lo[:], atm[:], a_hi[:], ALU.subtract),
                            waits=[at_a], inc=True)
                        fin_vt[0] = vt
                        for srct, dstt in ((a_hi, afh), (a_lo, afl)):
                            for rl in range(2):
                                rnd = 2 * ch + rl
                                fp = fin_ppt[0] % 2
                                fin_ppt[0] += 1
                                pptv = ppt[fp]
                                tt = None
                                for q4 in range(4):
                                    dc = 4 * rl + q4
                                    tt = op("T", lambda e, srct=srct, dc=dc,
                                            q4=q4, pptv=pptv: e.matmul(
                                                pptv[:, 128 * q4:
                                                     128 * (q4 + 1)],
                                                srct[:, 128 * dc:
                                                     128 * (dc + 1)],
                                                c_idb[:], is_transpose=True,
                                                start=(q4 == 0),
                                                stop=(q4 == 3)),
                                            waits=[vt, ppt_free[fp]]
                                            if q4 == 0 else [], inc=(q4 == 3))
                                af_free = tt
                                ppt_free[fp] = op(
                                    "A", lambda e, pptv=pptv: e.copy(
                                        astg[:], pptv[:, 0:512]),
                                    waits=[tt, astg_free], inc=True)
                                astg_free = dma("G", dstt.rearrange(
                                    "(c p) t -> p c t", p=128)[
                                        :, 4 * rnd:4 * rnd + 4,
                                        128 * m:128 * (m + 1)],
                                    astg.ap().rearrange("p (c t) -> p c t",
                                                        c=4),
                                    waits=[ppt_free[fp]], stream="astg")
            bd.barrier()
            bd.flush()

        NCHK = OWN // 512
        for u in range(NCHK):
            t0 = 512 * u
            with contextlib.ExitStack() as ph:
                psb = lambda n, s, d: ph.enter_context(
                    nc.sbuf_tensor(bd.uid(n), s, d))
                pA = ph.enter_context(
                    nc.psum_tensor(bd.uid("p5a"), [128, 2048], F32))
                pB = ph.enter_context(
                    nc.psum_tensor(bd.uid("p5b"), [128, 2048], F32))
                x2 = psb("x2", [128, DCH * 512], F32)
                h2h = psb("h2h", [128, DCH * 512], BF16)
                h2l = psb("h2l", [128, DCH * 512], BF16)
                lw_h = [psb("lwh%d" % i, [128, 512], BF16) for i in range(2)]
                lw_l = [psb("lwl%d" % i, [128, 512], BF16) for i in range(2)]
                rh_h = [psb("rhh%d" % i, [128, 512], BF16) for i in range(2)]
                rh_l = [psb("rhl%d" % i, [128, 512], BF16) for i in range(2)]
                r0b = psb("r0b", [128, 2048], F32)
                wf_h = [psb("wfh%d" % i, [128, DCH * 256], BF16)
                        for i in range(2)]
                wf_l = [psb("wfl%d" % i, [128, DCH * 256], BF16)
                        for i in range(2)]
                sg = psb("sg", [128, 512], F32)
                hsf = psb("hsf", [128, 512], F32)
                e_hi = psb("e_hi5", [128, 512], BF16)
                e_lo = psb("e_lo5", [128, 512], BF16)
                x2t = psb("x2t", [128, 2048], F32)
                h2t = psb("h2t", [128, 2048], F32)
                sst = psb("sst5", [128, 8], F32)
                ytile = psb("ytile", [128, 2048], F32)

                x2v = x2.ap().rearrange("p (c t) -> p c t", c=DCH)
                h2hv = h2h.ap().rearrange("p (c t) -> p c t", c=DCH)
                h2lv = h2l.ap().rearrange("p (c t) -> p c t", c=DCH)
                outv = x2v

                lw_free = [None, None]
                pA_free = None
                pB_free = None
                y_free = None
                vt5 = None
                for dcg in range(4):
                    pacc = pA if dcg % 2 == 0 else pB
                    pfree = pA_free if dcg % 2 == 0 else pB_free
                    ptok = None
                    for ec in range(32):
                        bb = ec % 2
                        gsh = glh if ec < 16 else afh
                        gsl = gll if ec < 16 else afl
                        er = 128 * (ec % 16)
                        l1 = dma("S", lw_h[bb][:],
                                 pTh[128 * ec:128 * (ec + 1),
                                     512 * dcg:512 * (dcg + 1)],
                                 waits=[lw_free[bb]], stream="lr%d" % bb)
                        l2 = dma("S", lw_l[bb][:],
                                 pTl[128 * ec:128 * (ec + 1),
                                     512 * dcg:512 * (dcg + 1)],
                                 stream="lr%d" % bb)
                        l3 = dma("S", rh_h[bb][:], gsh[er:er + 128,
                                                       t0:t0 + 512],
                                 stream="lr%d" % bb)
                        l4 = dma("S", rh_l[bb][:], gsl[er:er + 128,
                                                       t0:t0 + 512],
                                 stream="lr%d" % bb)
                        for dc4 in range(4):
                            pp5 = pacc[:, 512 * dc4:512 * (dc4 + 1)]
                            lhsh = lw_h[bb][:, 128 * dc4:128 * (dc4 + 1)]
                            lhsl = lw_l[bb][:, 128 * dc4:128 * (dc4 + 1)]
                            w0 = ([l1, l2, l3, l4, pfree]
                                  if dc4 == 0 else [])
                            op("T", lambda e, pp5=pp5, lhsh=lhsh, bb=bb,
                               f=(ec == 0): e.matmul(pp5, lhsh, rh_h[bb][:],
                                                     start=f, stop=False),
                               waits=w0)
                            op("T", lambda e, pp5=pp5, lhsh=lhsh, bb=bb:
                               e.matmul(pp5, lhsh, rh_l[bb][:], start=False,
                                        stop=False))
                            ptok = op("T", lambda e, pp5=pp5, lhsl=lhsl,
                                      bb=bb, l=(ec == 31): e.matmul(
                                          pp5, lhsl, rh_h[bb][:], start=False,
                                          stop=l),
                                      inc=(dc4 == 3))
                        lw_free[bb] = ptok
                    ldr = dma("S",
                              r0b.ap().rearrange("p (c t) -> p c t", c=4),
                              r0s3[:, 4 * dcg:4 * dcg + 4, t0:t0 + 512],
                              waits=[vt5], stream="r0b")
                    vt5 = None
                    for dc4 in range(4):
                        dc = 4 * dcg + dc4
                        vt5 = op("V", lambda e, pacc=pacc, dc4=dc4, dc=dc:
                                 e.scalar_tensor_tensor(
                                     x2v[:, dc, :],
                                     pacc[:, 512 * dc4:512 * (dc4 + 1)], 1.0,
                                     r0b[:, 512 * dc4:512 * (dc4 + 1)],
                                     ALU.mult, ALU.add),
                                 waits=[ptok, ldr], inc=(dc4 == 3))
                    if dcg % 2 == 0:
                        pA_free = vt5
                    else:
                        pB_free = vt5
                x2tok = vt5

                h2tok = None
                for psub in range(4):
                    tt = None
                    for rnd in range(2):
                        for cc in range(8):
                            c = 8 * rnd + cc
                            sl = 8 * rnd + cc
                            tt = op("T", lambda e, c=c, sl=sl, psub=psub:
                                    e.matmul(
                                        pA[:, 128 * sl:128 * (sl + 1)],
                                        x2v[:, c, 128 * psub:128 * (psub + 1)],
                                        c_idf[:], is_transpose=True,
                                        start=(sl % 4 == 0),
                                        stop=(sl % 4 == 3)),
                                    waits=[x2tok, pA_free]
                                    if sl == 0 else [], inc=(sl == 15))
                    pA_free = op("A", lambda e: e.copy(x2t[:], pA[:]),
                                 waits=[tt], sync_after=True)
                    vt = ln_tm(x2t, h2t, sst, extra_wait=[pA_free])
                    tt = None
                    for rnd in range(2):
                        for cc in range(8):
                            c = 8 * rnd + cc
                            sl = 8 * rnd + cc
                            tt = op("T", lambda e, c=c, sl=sl: e.matmul(
                                pB[:, 128 * sl:128 * (sl + 1)],
                                h2t[:, 128 * c:128 * (c + 1)], c_idf[:],
                                is_transpose=True, start=(sl % 4 == 0),
                                stop=(sl % 4 == 3)),
                                waits=[vt, pB_free] if sl == 0 else [],
                                inc=(sl == 15))
                    pBv = pB.ap().rearrange("p (c t) -> p c t", c=DCH)
                    at_h2 = op("A", lambda e, psub=psub: e.copy(
                        h2hv[:, :, 128 * psub:128 * (psub + 1)], pBv),
                        waits=[tt], sync_after=True)
                    h2tok = op("V", lambda e, psub=psub: e.tensor_tensor(
                        h2lv[:, :, 128 * psub:128 * (psub + 1)], pBv,
                        h2hv[:, :, 128 * psub:128 * (psub + 1)],
                        ALU.subtract), waits=[at_h2], inc=True)
                    pB_free = h2tok

                wf_free = [None, None]
                hs_free = None
                sg_free = None
                for og in range(22):
                    construct_step(48 if og < 16 else 24)
                    ocs = [i for i in (2 * og, 2 * og + 1) if i < ICH]
                    noc = len(ocs)
                    toks = {}
                    for mat, (mh, ml) in ((0, (wgh, wgl)), (1, (wuh, wul))):
                        par = (2 * og + mat) % 2
                        lt = None
                        for dc in range(DCH):
                            lt = dma("S", wf_h[par][:, dc * 256:
                                                    dc * 256 + 128 * noc],
                                     mh[128 * dc:128 * (dc + 1),
                                        256 * og:256 * og + 128 * noc],
                                     waits=[wf_free[par]],
                                     stream="wf%d" % par)
                            lt = dma("S", wf_l[par][:, dc * 256:
                                                    dc * 256 + 128 * noc],
                                     ml[128 * dc:128 * (dc + 1),
                                        256 * og:256 * og + 128 * noc],
                                     stream="wf%d" % par)
                        pacc = pA if mat == 0 else pB
                        pfree = pA_free if mat == 0 else pB_free
                        mmt = None
                        for oi, oc in enumerate(ocs):
                            po = pacc[:, 512 * oi:512 * (oi + 1)]
                            first = True
                            for dc in range(DCH):
                                lh = wf_h[par][:, (dc * 2 + oi) * 128:
                                               (dc * 2 + oi + 1) * 128]
                                llo = wf_l[par][:, (dc * 2 + oi) * 128:
                                                (dc * 2 + oi + 1) * 128]
                                rhh = h2hv[:, dc, :]
                                rhl = h2lv[:, dc, :]
                                w0 = ([lt, h2tok, pfree] if first else [])
                                op("T", lambda e, po=po, lh=lh, rhh=rhh,
                                   f=first: e.matmul(po, lh, rhh, start=f,
                                                     stop=False), waits=w0)
                                first = False
                                op("T", lambda e, po=po, lh=lh, rhl=rhl:
                                   e.matmul(po, lh, rhl, start=False,
                                            stop=False))
                                mmt = op("T", lambda e, po=po, llo=llo,
                                         rhh=rhh, l=(dc == DCH - 1):
                                         e.matmul(po, llo, rhh, start=False,
                                                  stop=l),
                                         inc=(dc == DCH - 1))
                        toks[mat] = mmt
                        wf_free[par] = mmt
                    gate_tok, up_tok = toks[0], toks[1]
                    vt6 = None
                    for oi, oc in enumerate(ocs):
                        at = op("A", lambda e, oi=oi: e.activation(
                            sg[:], pA[:, 512 * oi:512 * (oi + 1)], AF.Silu),
                            waits=[gate_tok, sg_free], inc=True)
                        sg_free = op("V", lambda e, oi=oi: e.tensor_tensor(
                            hsf[:], pB[:, 512 * oi:512 * (oi + 1)], sg[:],
                            ALU.mult), waits=[at, up_tok], sync_after=True)
                        at_e5 = op("A", lambda e: e.copy(e_hi[:], hsf[:]),
                                   waits=[hs_free, sg_free], sync_after=True)
                        vt6 = op("V", lambda e: e.tensor_tensor(
                            e_lo[:], hsf[:], e_hi[:], ALU.subtract),
                            waits=[at_e5], sync_after=True)
                        dma("G", hsh[128 * oc:128 * (oc + 1),
                                     t0:t0 + 512], e_hi[:], waits=[vt6],
                            stream="hsout")
                        hs_free = dma("G", hsl[128 * oc:128 * (oc + 1),
                                              t0:t0 + 512], e_lo[:],
                                      stream="hsout")
                    pA_free = vt6
                    pB_free = vt6

                dwn_tok = cstate["busy"] if u == 0 else None
                out_tok = None
                for dcg in range(4):
                    pacc = pA if dcg % 2 == 0 else pB
                    pfree = pA_free if dcg % 2 == 0 else pB_free
                    dtok = None
                    for ic in range(ICH):
                        bb = ic % 2
                        l1 = dma("S", lw_h[bb][:],
                                 wdh[128 * ic:128 * (ic + 1),
                                     512 * dcg:512 * (dcg + 1)],
                                 waits=[lw_free[bb], dwn_tok],
                                 stream="lr%d" % bb)
                        l2 = dma("S", lw_l[bb][:],
                                 wdl[128 * ic:128 * (ic + 1),
                                     512 * dcg:512 * (dcg + 1)],
                                 stream="lr%d" % bb)
                        l3 = dma("S", rh_h[bb][:],
                                 hsh[128 * ic:128 * (ic + 1), t0:t0 + 512],
                                 waits=[hs_free], stream="lr%d" % bb)
                        l4 = dma("S", rh_l[bb][:],
                                 hsl[128 * ic:128 * (ic + 1), t0:t0 + 512],
                                 stream="lr%d" % bb)
                        for dc4 in range(4):
                            pp5 = pacc[:, 512 * dc4:512 * (dc4 + 1)]
                            lhsh = lw_h[bb][:, 128 * dc4:128 * (dc4 + 1)]
                            lhsl = lw_l[bb][:, 128 * dc4:128 * (dc4 + 1)]
                            w0 = ([l1, l2, l3, l4, pfree] if dc4 == 0 else [])
                            op("T", lambda e, pp5=pp5, lhsh=lhsh, bb=bb,
                               f=(ic == 0): e.matmul(pp5, lhsh, rh_h[bb][:],
                                                     start=f, stop=False),
                               waits=w0)
                            op("T", lambda e, pp5=pp5, lhsh=lhsh, bb=bb:
                               e.matmul(pp5, lhsh, rh_l[bb][:], start=False,
                                        stop=False))
                            dtok = op("T", lambda e, pp5=pp5, lhsl=lhsl,
                                      bb=bb, l=(ic == ICH - 1): e.matmul(
                                          pp5, lhsl, rh_h[bb][:],
                                          start=False, stop=l),
                                      inc=(dc4 == 3))
                        lw_free[bb] = dtok
                    for dc4 in range(4):
                        dc = 4 * dcg + dc4
                        out_tok = op("V", lambda e, pacc=pacc, dc4=dc4, dc=dc:
                                     e.scalar_tensor_tensor(
                                         outv[:, dc, :],
                                         pacc[:, 512 * dc4:512 * (dc4 + 1)],
                                         1.0, x2v[:, dc, :], ALU.mult,
                                         ALU.add), waits=[dtok],
                                     inc=(dc4 == 3))
                    if dcg % 2 == 0:
                        pA_free = out_tok
                    else:
                        pB_free = out_tok

                for psub in range(4):
                    tt = None
                    for rnd in range(2):
                        for cc in range(8):
                            c = 8 * rnd + cc
                            sl = 8 * rnd + cc
                            tt = op("T", lambda e, c=c, sl=sl, psub=psub:
                                    e.matmul(
                                        pA[:, 128 * sl:128 * (sl + 1)],
                                        outv[:, c,
                                             128 * psub:128 * (psub + 1)],
                                        c_idf[:], is_transpose=True,
                                        start=(sl % 4 == 0),
                                        stop=(sl % 4 == 3)),
                                    waits=[out_tok, pA_free]
                                    if sl == 0 else [], inc=(sl == 15))
                    pA_free = op("A", lambda e: e.copy(
                        ytile[:], pA[:]), waits=[tt, y_free], inc=True)
                    y_free = dma("G", y[t0 + 128 * psub:
                                        t0 + 128 * (psub + 1), :],
                                 ytile[:], waits=[pA_free], stream="yout")
                bd.barrier()
                bd.flush()

    assert cstate["i"] >= len(cjobs), (cstate["i"], len(cjobs))
    bd.st.close()
    return nc


_CACHE = {}


def _get_nc(SB):
    if SB not in _CACHE:
        _CACHE[SB] = build_nc(SB)
    return _CACHE[SB]


def make_in_maps(SB, x_full, expand, project, gidx, uidx, didx, lut, pm_s,
                 gamma, beta):
    OWN = SB // 4
    eT = np.ascontiguousarray(expand.T)
    pT = np.ascontiguousarray(project.T)
    gT = np.ascontiguousarray(gidx.T).astype(ml_dtypes.bfloat16)
    uT = np.ascontiguousarray(uidx.T).astype(ml_dtypes.bfloat16)
    dT = np.ascontiguousarray(didx.T).astype(ml_dtypes.bfloat16)
    lut16 = np.ascontiguousarray(lut.reshape(1, 16))
    posm = np.ascontiguousarray(np.asarray(pm_s, np.float32).reshape(1, 1))
    gam = np.ascontiguousarray(gamma.reshape(DCH, 128).T)
    bet = np.ascontiguousarray(beta.reshape(DCH, 128).T)
    ones_ = np.ones((1, 128), np.float32)
    idf = np.eye(128, dtype=np.float32)
    idb = np.eye(128, dtype=BF)
    fi = np.arange(512, dtype=np.float32)
    pi = np.arange(128, dtype=np.float32)
    ramp = fi[None, :] - pi[:, None]
    csc = np.zeros((128, 4), np.float32)
    csc[:, 0] = EPS
    csc[:, 1] = 1.0
    jneg = np.tile(np.arange(16, dtype=np.float32), (128, 1))

    in_maps = []
    for c in range(8):
        bidx, r = divmod(c, 4)
        xbat = x_full[bidx]
        own_rows = np.concatenate(
            [np.arange(512 * m + 128 * r, 512 * m + 128 * r + 128)
             for m in range(SB // 512)])
        xo = np.ascontiguousarray(xbat[own_rows])
        keepm = (fi[None, :] <= 128 * r + pi[:, None]).astype(np.float32)
        dtab = np.zeros((128, 16), np.float32)
        for k in range(16):
            dtab[:, k] = -512.0 * k - 128.0 * r
        in_maps.append({
            "xb": np.ascontiguousarray(xbat),
            "xo": xo,
            "xoT": np.ascontiguousarray(xo.T),
            "eT": eT, "pT": pT, "gidxT": gT, "uidxT": uT, "didxT": dT,
            "lut16": lut16, "posm": posm, "gam": gam, "bet": bet,
            "ones1": ones_, "idf": idf, "idb": idb,
            "ramp": np.ascontiguousarray(ramp),
            "keep": keepm, "dtab": dtab, "csc": csc, "jneg": jneg,
        })
    return in_maps, [
        np.concatenate([np.arange(512 * m + 128 * (c % 4),
                                  512 * m + 128 * (c % 4) + 128)
                        for m in range(SB // 512)]) for c in range(8)]


def kernel(L0_attn_x, L0_attn_ln_gamma, L0_attn_ln_beta, L0_attn_expand,
           L0_attn_project, L0_attn_pos_mult, L0_ffn_lut, L0_ffn_gate_idx,
           L0_ffn_up_idx, L0_ffn_down_idx):
    SB = L0_attn_x.shape[1]
    x = np.ascontiguousarray(np.asarray(L0_attn_x, np.float32))
    nc = _get_nc(SB)
    in_maps, rows = make_in_maps(
        SB, x, np.asarray(L0_attn_expand, np.float32),
        np.asarray(L0_attn_project, np.float32),
        np.asarray(L0_ffn_gate_idx, np.int32),
        np.asarray(L0_ffn_up_idx, np.int32),
        np.asarray(L0_ffn_down_idx, np.int32),
        np.asarray(L0_ffn_lut, np.float32), L0_attn_pos_mult,
        np.asarray(L0_attn_ln_gamma, np.float32),
        np.asarray(L0_attn_ln_beta, np.float32))
    res = run_bass_kernel_spmd(nc, in_maps, core_ids=list(range(8)))
    out = np.empty((B, SB, D), np.float32)
    for c in range(8):
        out[c // 4][rows[c]] = res.results[c]["y"]
    return out


# version 18
# speedup vs baseline: 1.0881x; 1.0881x over previous
import contextlib

import numpy as np
import ml_dtypes

import concourse.bass as bass
import concourse.mybir as mybir
from concourse.bass_utils import run_bass_kernel_spmd

F32 = mybir.dt.float32
BF16 = mybir.dt.bfloat16
I32 = mybir.dt.int32
I16 = mybir.dt.int16
AF = mybir.ActivationFunctionType
ALU = mybir.AluOpType
AX = mybir.AxisListType

B = 2
D = 2048
E = 8704
EXPAND = 4096
INTER = 5504
DCH = D // 128
ICH = INTER // 128
EPS = 1e-5
SCALE_QK = 0.0625

BF = ml_dtypes.bfloat16


class Builder:
    def __init__(self, SB):
        self.SB = SB
        self.nc = bass.Bass("TRN2", target_bir_lowering=False)
        self.st = contextlib.ExitStack()
        self.gen = 0
        self.bar_cnt = 0
        self.bar = self.st.enter_context(self.nc.semaphore("bar"))
        self.sems = {}
        self.waited = {e: {} for e in "STVAG"}
        self.ops = []
        self.sem_serial = 0
        self.uid_n = 0
        self.new_phase_sems()

    def uid(self, n):
        self.uid_n += 1
        return "%s_u%d" % (n, self.uid_n)

    def new_phase_sems(self):
        self.sem_serial += 1
        self.sems = {}
        for k in ("T", "V", "A", "G"):
            h = self.st.enter_context(
                self.nc.semaphore("s%d%s" % (self.sem_serial, k)))
            self.sems[k] = [h, 0]

    def _qsem(self, k):
        if k not in self.sems:
            self.sem_serial += 1
            h = self.st.enter_context(
                self.nc.semaphore("s%d%s" % (self.sem_serial,
                                             k.replace(":", "_"))))
            self.sems[k] = [h, 0]
        return self.sems[k]

    def last_tok(self, k):
        ent = self.sems[k]
        return (ent[0], ent[1], self.gen)

    def _tok(self, k, amt):
        ent = self._qsem(k) if ":" in k else self.sems[k]
        if ent[1] + amt > 48000:
            self.sem_serial += 1
            h = self.st.enter_context(
                self.nc.semaphore("s%d%s" % (self.sem_serial, k)))
            self.sems[k] = ent = [h, 0]
        ent[1] += amt
        return (ent[0], ent[1], self.gen)

    def _emit_waits(self, eng, waits):
        best = {}
        for t in waits:
            if t is None:
                continue
            h, val, g = t
            if g < self.gen:
                continue
            if best.get(h, -1) < val:
                best[h] = val
        use = []
        wd = self.waited[eng]
        for h, val in best.items():
            if wd.get(h, -1) < val:
                wd[h] = val
                use.append((h, val))
        return use

    def op(self, eng, fn, waits=(), inc=False, sync_after=False):
        use = self._emit_waits(eng, waits)
        tok = self._tok(eng, 1) if (inc or sync_after) else None
        fence = tok if sync_after else None
        if fence is not None:
            self.waited[eng][fence[0]] = fence[1]

        def run(e, fn=fn, use=tuple(use), tok=tok, fence=fence):
            for h, val in use:
                e.wait_ge(h, val)
            ins = fn(e)
            if tok is not None:
                ins.then_inc(tok[0], 1)
            if fence is not None:
                e.wait_ge(fence[0], fence[1])

        self.ops.append((eng, run))
        return tok

    def dma(self, q, dst, src, waits=(), stream="snk"):
        use = self._emit_waits(q, waits)
        tok = self._tok("q" + q + ":" + stream, 16)

        def run(e, dst=dst, src=src, use=tuple(use), tok=tok):
            for h, val in use:
                e.wait_ge(h, val)
            e.dma_start(dst, src).then_inc(tok[0], 16)

        self.ops.append((q, run))
        return tok

    def barrier(self):
        for q in ("S", "G"):
            pref = "q" + q + ":"
            for k, ent in list(self.sems.items()):
                if not k.startswith(pref):
                    continue
                if ent[1] > self.waited[q].get(ent[0], -1):
                    self.waited[q][ent[0]] = ent[1]
                    self.ops.append(
                        (q, lambda e, h=ent[0], v=ent[1]: e.wait_ge(h, v)))
        self.bar_cnt += 5
        tgt = self.bar_cnt
        for eng in "STVAG":
            def run(e, tgt=tgt):
                e.sem_inc(self.bar, 1)
                e.wait_ge(self.bar, tgt)
            self.ops.append((eng, run))
        self.gen += 1
        self.new_phase_sems()

    def flush(self):
        with self.nc.Block() as blk:
            for ek, meth in (("S", blk.sync), ("T", blk.tensor),
                             ("V", blk.vector), ("A", blk.scalar),
                             ("G", blk.gpsimd)):
                items = [f for (k, f) in self.ops if k == ek]
                if items:
                    meth(lambda e, items=items: [f(e) for f in items])
        self.ops = []


def build_nc(SB):
    bd = Builder(SB)
    nc = bd.nc
    OWN = SB // 4
    NSLOT = SB // 512
    NSUP = SB // 1024
    NSUP_OWN = OWN // 1024
    op, dma = bd.op, bd.dma

    ap = lambda *a, **k: nc.dram_tensor(*a, **k).ap()
    xb = ap("xb", [SB, D], F32, kind="ExternalInput")
    xo = ap("xo", [OWN, D], F32, kind="ExternalInput")
    xoT = ap("xoT", [D, OWN], F32, kind="ExternalInput")
    eT = ap("eT", [D, E], F32, kind="ExternalInput")
    pT = ap("pT", [EXPAND, D], F32, kind="ExternalInput")
    gidxT = ap("gidxT", [D, INTER], BF16, kind="ExternalInput")
    uidxT = ap("uidxT", [D, INTER], BF16, kind="ExternalInput")
    didxT = ap("didxT", [INTER, D], BF16, kind="ExternalInput")
    lut16 = ap("lut16", [1, 16], F32, kind="ExternalInput")
    posm = ap("posm", [1, 1], F32, kind="ExternalInput")
    gam = ap("gam", [128, DCH], F32, kind="ExternalInput")
    bet = ap("bet", [128, DCH], F32, kind="ExternalInput")
    ones1 = ap("ones1", [1, 128], F32, kind="ExternalInput")
    idf = ap("idf", [128, 128], F32, kind="ExternalInput")
    idb = ap("idb", [128, 128], BF16, kind="ExternalInput")
    ramp = ap("ramp", [128, 512], F32, kind="ExternalInput")
    keep = ap("keep", [128, 512], F32, kind="ExternalInput")
    dtab = ap("dtab", [128, 16], F32, kind="ExternalInput")
    csc = ap("csc", [128, 4], F32, kind="ExternalInput")
    jneg = ap("jneg", [128, 16], F32, kind="ExternalInput")
    y = ap("y", [OWN, D], F32, kind="ExternalOutput")

    eTh = ap("eTh", [D, E], BF16)
    eTl = ap("eTl", [D, E], BF16)
    pTh = ap("pTh", [EXPAND, D], BF16)
    pTl = ap("pTl", [EXPAND, D], BF16)
    wgh = ap("wgh", [D, INTER], BF16)
    wgl = ap("wgl", [D, INTER], BF16)
    wuh = ap("wuh", [D, INTER], BF16)
    wul = ap("wul", [D, INTER], BF16)
    wdh = ap("wdh", [INTER, D], BF16)
    wdl = ap("wdl", [INTER, D], BF16)
    kh = ap("kh", [256, SB], BF16)
    kl = ap("kl", [256, SB], BF16)
    vh = ap("vh", [SB, D], BF16)
    vl = ap("vl", [SB, D], BF16)
    glh = ap("glh", [D, OWN], BF16)
    gll = ap("gll", [D, OWN], BF16)
    afh = ap("afh", [D, OWN], BF16)
    afl = ap("afl", [D, OWN], BF16)
    r0s = ap("r0s", [D, OWN], F32)
    hsh = ap("hsh", [INTER, OWN], BF16)
    hsl = ap("hsl", [INTER, OWN], BF16)

    xoT3 = xoT.rearrange("(c p) t -> p c t", p=128)
    r0s3 = r0s.rearrange("(c p) t -> p c t", p=128)

    sbufg = lambda n, s, d: bd.st.enter_context(nc.sbuf_tensor(n, s, d))
    c_ones = sbufg("c_ones", [1, 128], F32)
    c_idf = sbufg("c_idf", [128, 128], F32)
    c_idb = sbufg("c_idb", [128, 128], BF16)
    c_csc = sbufg("c_csc", [128, 4], F32)
    c_jneg = sbufg("c_jneg", [128, 16], F32)
    c_gam = sbufg("c_gam", [128, DCH], F32)
    c_bet = sbufg("c_bet", [128, DCH], F32)
    c_lut = sbufg("c_lut", [1, 16], F32)
    c_posm = sbufg("c_posm", [1, 1], F32)
    luthb = sbufg("luthb", [128, 16], F32)
    lutlb = sbufg("lutlb", [128, 16], F32)
    lutbs = sbufg("lutbs", [128, 16], BF16)
    msc = sbufg("msc", [128, 1], F32)
    mR = sbufg("mR", [128, 512], F32)
    tdiag = sbufg("tdiag", [128, 512], F32)
    mdt = sbufg("mdt", [128, 16], F32)
    smt = sbufg("smt", [128, 8], F32)
    cb_idx = sbufg("cb_idx", [128, 1024], BF16)
    cb_acc = sbufg("cb_acc", [128, 1024], F32)
    cb_msk = [sbufg("cb_msk%d" % i, [128, 1024], BF16) for i in range(3)]
    lutfg = sbufg("lutfg", [128, 16], F32)
    cb_hi = sbufg("cb_hi", [128, 1024], BF16)
    cb_lo = sbufg("cb_lo", [128, 1024], BF16)

    cjobs = []
    for c0 in range(0, INTER, 1024):
        w = min(1024, INTER - c0)
        for (src, dh, dl) in ((gidxT, wgh, wgl), (uidxT, wuh, wul)):
            for rr in range(0, D, 128):
                cjobs.append((src, rr, c0, w, dh, dl))
    for c0 in range(0, D, 1024):
        for rr in range(0, INTER, 128):
            cjobs.append((didxT, rr, c0, 1024, wdh, wdl))
    cstate = {"i": 0, "j": 0, "busy": None}
    assert NSUP_OWN >= 1, "SB must be >= 4096"
    _HS = NSLOT // 2
    _visits = _HS * _HS + _HS * (_HS + 1)
    _p3cap = (5 * _visits) // 16
    _rem = len(cjobs) - _p3cap - 38
    CQ2 = 28
    CQ1 = max(0, -(-max(0, _rem - CQ2 * NSUP_OWN) // NSUP))

    def construct_step(nsteps):
        for _ in range(nsteps):
            i = cstate["i"]
            if i >= len(cjobs):
                return
            j = cstate["j"]
            src_, rr, c0, w, dh, dl = cjobs[i]
            h1 = w // 2

            def mk(jj, lo, hi, waits):
                return op("V", lambda e, jj=jj, lo=lo, hi=hi: e.tensor_scalar(
                    cb_msk[jj % 3][:, lo:hi], cb_idx[:, lo:hi],
                    c_jneg[:, jj:jj + 1], None, ALU.is_equal),
                    waits=waits, inc=True)

            if j == 0:
                ld = dma("G", cb_idx[:, 0:w], src_[rr:rr + 128, c0:c0 + w],
                         waits=[cstate["busy"], cstate.get("mlast")],
                         stream="cidx")
                sh = cstate.get("s_hist") or {0: None, 1: None, 2: None}
                mtokA = {}
                mtokB = {}
                for jj in range(3):
                    mtokA[jj] = mk(jj, 0, h1,
                                   [ld, sh[jj]] if jj == 0 else [sh[jj]])
                    mtokB[jj] = mk(jj, h1, w, [sh[jj]])
                cstate["mtokA"] = mtokA
                cstate["mtokB"] = mtokB
                cstate["s_hist"] = sh
                cstate["sA"] = None
                cstate["sB"] = None
            mtokA = cstate["mtokA"]
            mtokB = cstate["mtokB"]
            sh = cstate["s_hist"]
            mb = cb_msk[j % 3]
            if j == 0:
                sA = op("V", lambda e, h1=h1, mb=mb: e.tensor_scalar(
                    cb_acc[:, 0:h1], mb[:, 0:h1], lutfg[:, 0:1], None,
                    ALU.mult), waits=[mtokA[0], cstate.get("hi_prev")],
                    inc=True)
                sB = op("V", lambda e, h1=h1, w=w, mb=mb: e.tensor_scalar(
                    cb_acc[:, h1:w], mb[:, h1:w], lutfg[:, 0:1], None,
                    ALU.mult), waits=[mtokB[0]], inc=True)
            else:
                sA = op("V", lambda e, h1=h1, j=j, mb=mb:
                        e.scalar_tensor_tensor(
                            cb_acc[:, 0:h1], mb[:, 0:h1],
                            lutfg[:, j:j + 1], cb_acc[:, 0:h1],
                            ALU.mult, ALU.add),
                        waits=[mtokA[j], cstate["sA"]], inc=True)
                sB = op("V", lambda e, h1=h1, w=w, j=j, mb=mb:
                        e.scalar_tensor_tensor(
                            cb_acc[:, h1:w], mb[:, h1:w],
                            lutfg[:, j:j + 1], cb_acc[:, h1:w],
                            ALU.mult, ALU.add),
                        waits=[mtokB[j], cstate["sB"]], inc=True)
            if j + 3 < 16:
                mtokA[j + 3] = mk(j + 3, 0, h1, [sA])
                mtokB[j + 3] = mk(j + 3, h1, w, [sB])
            sh[j % 3] = sB
            cstate["sA"] = sA
            cstate["sB"] = sB
            if j < 15:
                cstate["j"] = j + 1
                continue
            cstate["mlast"] = mtokB[15]
            hi = op("A", lambda e, w=w: e.copy(cb_hi[:, 0:w], cb_acc[:, 0:w]),
                    waits=[sB, cstate["busy"]], sync_after=True)
            lo = op("V", lambda e, w=w: e.tensor_tensor(
                cb_lo[:, 0:w], cb_acc[:, 0:w], cb_hi[:, 0:w], ALU.subtract),
                waits=[hi, cstate["busy"]], inc=True)
            cstate["hi_prev"] = hi
            dma("G", dh[rr:rr + 128, c0:c0 + w], cb_hi[:, 0:w],
                waits=[lo], stream="cwb")
            cstate["busy"] = dma("G", dl[rr:rr + 128, c0:c0 + w],
                                 cb_lo[:, 0:w], stream="cwb")
            cstate["i"] = i + 1
            cstate["j"] = 0

    def construct_quantum(njobs):
        construct_step(16 * njobs)

    with contextlib.ExitStack() as ph:
        psb = lambda n, s, d: ph.enter_context(nc.sbuf_tensor(n, s, d))
        pp = ph.enter_context(nc.psum_tensor("pp0", [128, 512], F32))
        c_ramp = psb("c_ramp", [128, 512], F32)
        c_keep = psb("c_keep", [128, 512], F32)
        c_dtab = psb("c_dtab", [128, 16], F32)
        ldtok = None
        for dst, src in ((c_ones, ones1), (c_idf, idf), (c_idb, idb),
                         (c_ramp, ramp), (c_keep, keep), (c_dtab, dtab),
                         (c_csc, csc), (c_jneg, jneg), (c_gam, gam),
                         (c_bet, bet),
                         (c_lut, lut16), (c_posm, posm)):
            ldtok = dma("S", dst[:], src, stream="c")
        op("T", lambda e: e.matmul(pp[:, 0:16], c_ones[:], c_lut[:],
                                   start=True, stop=True), waits=[ldtok])
        ttok = op("T", lambda e: e.matmul(pp[:, 16:17], c_ones[:], c_posm[:],
                                          start=True, stop=True), inc=True)
        lutf = lutfg
        pm = psb("pm", [128, 1], F32)
        op("V", lambda e: e.tensor_copy(lutf[:], pp[:, 0:16]), waits=[ttok])
        op("V", lambda e: e.tensor_copy(pm[:], pp[:, 16:17]), sync_after=True)
        op("V", lambda e: e.tensor_copy(lutbs[:], lutf[:]), sync_after=True)
        op("V", lambda e: e.tensor_copy(luthb[:], lutbs[:]), sync_after=True)
        op("V", lambda e: e.tensor_tensor(lutlb[:], lutf[:], luthb[:],
                                          ALU.subtract), sync_after=True)
        op("V", lambda e: e.tensor_copy(lutbs[:], lutlb[:]), sync_after=True)
        vt = op("V", lambda e: e.tensor_copy(lutlb[:], lutbs[:]),
                sync_after=True)
        op("A", lambda e: e.activation(smt[:, 0:1], pm[:], AF.Abs),
           waits=[vt], sync_after=True)
        at = op("A", lambda e: e.activation(smt[:, 1:2], smt[:, 0:1], AF.Exp,
                                            scale=-1.0), sync_after=True)
        vt = op("V", lambda e: e.tensor_scalar(smt[:, 2:3], smt[:, 1:2], 1.0,
                                               None, ALU.add), waits=[at],
                sync_after=True)
        op("A", lambda e: e.activation(smt[:, 3:4], smt[:, 2:3], AF.Ln),
           waits=[vt], sync_after=True)
        at = op("A", lambda e: e.activation(smt[:, 4:5], pm[:], AF.Relu),
                sync_after=True)
        op("V", lambda e: e.tensor_tensor(msc[:], smt[:, 3:4], smt[:, 4:5],
                                          ALU.add), waits=[at], sync_after=True)
        op("V", lambda e: e.tensor_scalar(mR[:], c_ramp[:], msc[:], None,
                                          ALU.mult))
        op("V", lambda e: e.tensor_scalar(smt[:, 5:6], c_dtab[:, 0:1], msc[:],
                                          None, ALU.mult), sync_after=True)
        op("V", lambda e: e.tensor_scalar(tdiag[:], mR[:], smt[:, 5:6], None,
                                          ALU.add), sync_after=True)
        op("V", lambda e: e.tensor_tensor(tdiag[:], tdiag[:], c_keep[:],
                                          ALU.mult), sync_after=True)
        tneg = psb("tneg", [128, 512], F32)
        op("V", lambda e: e.tensor_scalar(tneg[:], c_keep[:], 1e30, -1e30,
                                          ALU.mult, ALU.add), sync_after=True)
        op("V", lambda e: e.tensor_tensor(tdiag[:], tdiag[:], tneg[:],
                                          ALU.add))
        op("V", lambda e: e.tensor_scalar(mdt[:], c_dtab[:], msc[:], None,
                                          ALU.mult), sync_after=True)

        spl_f = [psb("spl_f%d" % i, [128, E], F32) for i in range(2)]
        spl_h = [psb("spl_h%d" % i, [128, E], BF16) for i in range(2)]
        spl_l = [psb("spl_l%d" % i, [128, E], BF16) for i in range(2)]
        wb_tok = [None, None]

        def split_rows(srcF, dstH, dstL, nrow, ncol):
            for rr in range(0, nrow, 128):
                b = (rr // 128) % 2
                ld = dma("S", spl_f[b][:, 0:ncol], srcF[rr:rr + 128, :],
                         waits=[wb_tok[b]], stream="sp%d" % b)
                op("V", lambda e, b=b, n=ncol: e.tensor_copy(
                    spl_h[b][:, 0:n], spl_f[b][:, 0:n]), waits=[ld],
                    sync_after=True)
                vtk = op("V", lambda e, b=b, n=ncol: e.tensor_tensor(
                    spl_l[b][:, 0:n], spl_f[b][:, 0:n], spl_h[b][:, 0:n],
                    ALU.subtract), sync_after=True)
                dma("G", dstH[rr:rr + 128, :], spl_h[b][:, 0:ncol],
                    waits=[vtk], stream="spw%d" % b)
                wb_tok[b] = dma("G", dstL[rr:rr + 128, :], spl_l[b][:, 0:ncol],
                                stream="spw%d" % b)

        split_rows(eT, eTh, eTl, D, E)
        split_rows(pT, pTh, pTl, EXPAND, D)
        bd.barrier()
        bd.flush()

    def ln_tm(src, dst, sst, extra_wait=()):
        op("A", lambda e: e.activation(dst[:], src[:], AF.Identity,
                                       accum_out=sst[:, 0:1]),
           waits=list(extra_wait), sync_after=True)
        at = op("A", lambda e: e.activation(dst[:], src[:], AF.Square,
                                            accum_out=sst[:, 1:2]),
                sync_after=True)
        op("V", lambda e, sst=sst: e.tensor_scalar(
            sst[:, 2:3], sst[:, 0:1], -1.0 / D, None, ALU.mult), waits=[at],
            sync_after=True)
        op("V", lambda e, sst=sst: e.scalar_tensor_tensor(
            sst[:, 3:4], sst[:, 2:3], 1.0, sst[:, 2:3], ALU.mult, ALU.mult))
        op("V", lambda e, sst=sst: e.tensor_scalar(
            sst[:, 4:5], sst[:, 1:2], 1.0 / D, None, ALU.mult),
            sync_after=True)
        vt = op("V", lambda e, sst=sst: e.tensor_tensor(
            sst[:, 4:5], sst[:, 4:5], sst[:, 3:4], ALU.subtract),
            sync_after=True)
        at = op("A", lambda e, sst=sst: e.activation(
            sst[:, 5:6], sst[:, 4:5], AF.Sqrt, bias=c_csc[:, 0:1]),
            waits=[vt], sync_after=True)
        op("V", lambda e, sst=sst: e.reciprocal(sst[:, 6:7], sst[:, 5:6]),
           waits=[at], sync_after=True)
        return op("V", lambda e, sst=sst: e.tensor_scalar(
            dst[:], src[:], sst[:, 2:3], sst[:, 6:7], ALU.add, ALU.mult),
            sync_after=True)

    def expand_phase(x_src, nsup, is_own, cq):
        lb = 512 if is_own else 2560
        pb = 4608 if is_own else 6656
        qb = 0 if is_own else 256
        with contextlib.ExitStack() as ph:
            psb = lambda n, s, d: ph.enter_context(
                nc.sbuf_tensor(bd.uid(n), s, d))
            pf = ph.enter_context(nc.psum_tensor(bd.uid("pf"), [128, 3072], F32))
            pbf = ph.enter_context(nc.psum_tensor(bd.uid("pbf"), [128, 1024], BF16))
            xbuf = [psb("xb%d" % i, [128, D], F32) for i in range(2)]
            x0 = psb("x0", [128, D], F32)
            ht = psb("ht", [128, D], F32)
            hfp = psb("hfp", [128, D], F32)
            h_hi = psb("h_hi", [128, DCH * 1024], BF16)
            h_lo = psb("h_lo", [128, DCH * 1024], BF16)
            nwb = 1 if is_own else 2
            wb_h = [psb("wbh%d" % i, [128, DCH * 512], BF16)
                    for i in range(nwb)]
            wb_l = [psb("wbl%d" % i, [128, DCH * 512], BF16)
                    for i in range(nwb)]
            gbuf = psb("gbuf", [128, 512], F32)
            vf32 = psb("vf32", [128, 512], F32)
            e_hi = psb("e_hi", [128, 512], BF16)
            e_lo = psb("e_lo", [128, 512], BF16)
            rstg = psb("rstg", [128, 2048], F32) if is_own else None
            vstg_h = None if is_own else psb("vstg_h", [128, 2 * 4 * 256], BF16)
            vstg_l = None if is_own else psb("vstg_l", [128, 2 * 4 * 256], BF16)
            sst = psb("sst", [128, 8], F32)

            h_hi3 = h_hi.ap().rearrange("p (c t) -> p c t", c=DCH)
            h_lo3 = h_lo.ap().rearrange("p (c t) -> p c t", c=DCH)
            hfpv = hfp.ap().rearrange("p (c t) -> p c t", c=DCH)
            rstgv = (rstg.ap().rearrange("p (c t) -> p c t", c=DCH)
                     if is_own else None)

            ln_free = None
            wb_free = [None, None]

            bank_free = [None] * 4
            pbf_free = None
            e_free = None
            gb_free = None
            vstg_free = None
            rstg_free = None

            xb_free = [None, None]
            vt = None
            for s in range(nsup):
                h_tok = None
                for i in range(8):
                    t0 = 1024 * s + 128 * i
                    b = i % 2
                    vt = ln_tm(xbuf[b], x0, sst, extra_wait=[
                        dma("S", xbuf[b][:], x_src[t0:t0 + 128, :],
                            waits=[xb_free[b]], stream="x%d" % b), vt])
                    xb_free[b] = vt
                    vt = ln_tm(x0, ht, sst, extra_wait=[vt])
                    for rnd in range(2):
                        tt = None
                        for cc in range(8):
                            c = 8 * rnd + cc
                            tt = op("T", lambda e, c=c, cc=cc: e.matmul(
                                pf[:, 2048 + 128 * cc:2048 + 128 * (cc + 1)],
                                ht[:, 128 * c:128 * (c + 1)], c_idf[:],
                                is_transpose=True, start=(cc % 4 == 0),
                                stop=(cc % 4 == 3)),
                                waits=[vt, ln_free] if cc == 0 else [],
                                inc=(cc == 7))
                        for cc in range(8):
                            c = 8 * rnd + cc
                            ln_free = op("A", lambda e, c=c, cc=cc: e.activation(
                                hfp[:, 128 * c:128 * (c + 1)],
                                pf[:, 2048 + 128 * cc:2048 + 128 * (cc + 1)],
                                AF.Identity, bias=c_bet[:, c:c + 1],
                                scale=c_gam[:, c:c + 1]), waits=[tt],
                                inc=(cc == 7))
                    at_h = op("A", lambda e, i=i: e.copy(
                        h_hi3[:, :, 128 * i:128 * (i + 1)], hfpv),
                        waits=[ln_free], sync_after=True)
                    h_tok = op("V", lambda e, i=i: e.tensor_tensor(
                        h_lo3[:, :, 128 * i:128 * (i + 1)], hfpv,
                        h_hi3[:, :, 128 * i:128 * (i + 1)], ALU.subtract),
                        waits=[at_h], inc=True)
                    if is_own:
                        ldx = dma("S", rstgv[:, :, :],
                                  xoT3[:, :, t0:t0 + 128], waits=[rstg_free],
                                  stream="rx")
                        vt2 = None
                        for rnd in range(2):
                            tt = None
                            for cc in range(8):
                                c = 8 * rnd + cc
                                tt = op("T", lambda e, c=c, cc=cc: e.matmul(
                                    pf[:, 2048 + 128 * cc:
                                       2048 + 128 * (cc + 1)],
                                    x0[:, 128 * c:128 * (c + 1)], c_idf[:],
                                    is_transpose=True, start=(cc % 4 == 0),
                                    stop=(cc % 4 == 3)),
                                    waits=[h_tok, ln_free] if cc == 0 else [],
                                    inc=(cc == 7))
                            for cc in range(8):
                                c = 8 * rnd + cc
                                vt2 = op("V", lambda e, c=c, cc=cc:
                                         e.scalar_tensor_tensor(
                                             rstgv[:, c, :],
                                             pf[:, 2048 + 128 * cc:
                                                2048 + 128 * (cc + 1)], 1.0,
                                             rstgv[:, c, :], ALU.mult, ALU.add),
                                         waits=[tt, ldx], inc=(cc == 7))
                            ln_free = vt2
                        vt = vt2
                        rstg_free = dma(
                            "G", r0s3[:, :, t0:t0 + 128], rstgv[:, :, :],
                            waits=[vt2], stream="rstg")

                slabs = [[("kq", 0), ("kq", 1)]]
                for j0 in range(0, 16, 2):
                    slabs.append([("lin", j0), ("lin", j0 + 1),
                                  ("pg", j0), ("pg", j0 + 1)])

                for si, slab in enumerate(slabs):
                    par = (s * len(slabs) + si) % nwb
                    wbh_c = wb_h[par]
                    wbl_c = wb_l[par]
                    lt = None
                    for dc in range(DCH):
                        for ei0 in range(0, len(slab), 2):
                            kind, j = slab[ei0]
                            ecol = (qb + 128 * j if kind == "kq" else
                                    (lb if kind == "lin" else pb) + 128 * j)
                            lt = dma("S",
                                     wbh_c[:, (dc * 4 + ei0) * 128:
                                           (dc * 4 + ei0 + 2) * 128],
                                     eTh[128 * dc:128 * (dc + 1),
                                         ecol:ecol + 256],
                                     waits=[wb_free[par]],
                                     stream="wb%d" % par)
                            lt = dma("S",
                                     wbl_c[:, (dc * 4 + ei0) * 128:
                                           (dc * 4 + ei0 + 2) * 128],
                                     eTl[128 * dc:128 * (dc + 1),
                                         ecol:ecol + 256],
                                     stream="wb%d" % par)
                    for tb in range(2):
                        mm_tok = [None] * len(slab)
                        for ei in range(len(slab)):
                            po = pf[:, 512 * ei:512 * (ei + 1)]
                            first = True
                            for dc in range(DCH):
                                lh = wbh_c[:, (dc * 4 + ei) * 128:
                                           (dc * 4 + ei + 1) * 128]
                                llo = wbl_c[:, (dc * 4 + ei) * 128:
                                            (dc * 4 + ei + 1) * 128]
                                rh = h_hi3[:, dc, 512 * tb:512 * (tb + 1)]
                                rl = h_lo3[:, dc, 512 * tb:512 * (tb + 1)]
                                w0 = [lt, h_tok, bank_free[ei]] if first else []
                                op("T", lambda e, po=po, lh=lh, rh=rh, f=first:
                                   e.matmul(po, lh, rh, start=f, stop=False),
                                   waits=w0)
                                first = False
                                op("T", lambda e, po=po, lh=lh, rl=rl:
                                   e.matmul(po, lh, rl, start=False,
                                            stop=False))
                                mm_tok[ei] = op(
                                    "T", lambda e, po=po, llo=llo, rh=rh,
                                    l=(dc == DCH - 1): e.matmul(
                                        po, llo, rh, start=False, stop=l),
                                    inc=(dc == DCH - 1))
                        tg = 1024 * s + 512 * tb
                        if slab[0][0] == "kq":
                            for ei in range(2):
                                qc = slab[ei][1]
                                at_e = op("A", lambda e, ei=ei: e.copy(
                                    e_hi[:], pf[:, 512 * ei:512 * (ei + 1)]),
                                    waits=[mm_tok[ei], e_free],
                                    sync_after=True)
                                vt = op("V", lambda e, ei=ei: e.tensor_tensor(
                                    e_lo[:], pf[:, 512 * ei:512 * (ei + 1)],
                                    e_hi[:], ALU.subtract), waits=[at_e],
                                    sync_after=True)
                                if is_own:
                                    op("V", lambda e, qc=qc, tg=tg:
                                       e.tensor_copy(
                                           q_hi[:, qc * OWN + tg:
                                                qc * OWN + tg + 512], e_hi[:]))
                                    vt = op("V", lambda e, qc=qc, tg=tg:
                                            e.tensor_copy(
                                                q_lo[:, qc * OWN + tg:
                                                     qc * OWN + tg + 512],
                                                e_lo[:]), inc=True)
                                    bank_free[ei] = vt
                                    e_free = vt
                                else:
                                    dma("G", kh[128 * qc:128 * (qc + 1),
                                                tg:tg + 512], e_hi[:],
                                        waits=[vt], stream="eout")
                                    e_free = dma(
                                        "G", kl[128 * qc:128 * (qc + 1),
                                                tg:tg + 512], e_lo[:],
                                        stream="eout")
                                    bank_free[ei] = vt
                        else:
                            for pi in range(2):
                                j = slab[pi][1]
                                plin = pf[:, 512 * pi:512 * (pi + 1)]
                                ppg = pf[:, 512 * (2 + pi):512 * (3 + pi)]
                                at = op("A", lambda e, ppg=ppg: e.activation(
                                    gbuf[:], ppg, AF.Gelu),
                                    waits=[mm_tok[2 + pi], gb_free], inc=True)
                                gb_free = op(
                                    "V", lambda e, plin=plin: e.tensor_tensor(
                                        vf32[:], plin, gbuf[:], ALU.mult),
                                    waits=[at, mm_tok[pi]], sync_after=True)
                                at_e = op("A", lambda e: e.copy(e_hi[:],
                                                                 vf32[:]),
                                          waits=[e_free, gb_free],
                                          sync_after=True)
                                vt = op("V", lambda e: e.tensor_tensor(
                                    e_lo[:], vf32[:], e_hi[:], ALU.subtract),
                                    waits=[at_e], sync_after=True)
                                bank_free[pi] = vt
                                bank_free[2 + pi] = vt
                                if is_own:
                                    dma("G", glh[128 * j:128 * (j + 1),
                                                 tg:tg + 512], e_hi[:],
                                        waits=[vt], stream="eout")
                                    e_free = dma(
                                        "G", gll[128 * j:128 * (j + 1),
                                                 tg:tg + 512], e_lo[:],
                                        stream="eout")
                                else:
                                    tt = None
                                    for half, srct in ((0, e_hi), (1, e_lo)):
                                        for ks in range(4):
                                            sl = 4 * half + ks
                                            tt = op("T", lambda e, srct=srct,
                                                    ks=ks, sl=sl: e.matmul(
                                                        pbf[:, 128 * sl:
                                                            128 * (sl + 1)],
                                                        srct[:, 128 * ks:
                                                             128 * (ks + 1)],
                                                        c_idb[:],
                                                        is_transpose=True,
                                                        start=(sl == 0),
                                                        stop=(sl == 7)),
                                                    waits=[vt, pbf_free]
                                                    if sl == 0 else [],
                                                    inc=(sl == 7))
                                    e_free = tt
                                    jj = j % 2
                                    for ks in range(4):
                                        cb = (tb * 4 + ks) * 256 + 128 * jj
                                        op("V", lambda e, ks=ks, cb=cb:
                                           e.tensor_copy(
                                               vstg_h[:, cb:cb + 128],
                                               pbf[:, 128 * ks:
                                                   128 * (ks + 1)]),
                                           waits=[tt, vstg_free]
                                           if ks == 0 else [tt])
                                        pbf_free = op(
                                            "V", lambda e, ks=ks, cb=cb:
                                            e.tensor_copy(
                                                vstg_l[:, cb:cb + 128],
                                                pbf[:, 128 * (4 + ks):
                                                    128 * (5 + ks)]),
                                            inc=(ks == 3))
                                    if jj == 1:
                                        dvc = 128 * (j - 1)
                                        for ks in range(4):
                                            trow = tg + 128 * ks
                                            cb = (tb * 4 + ks) * 256
                                            dma("G", vh[trow:trow + 128,
                                                        dvc:dvc + 256],
                                                vstg_h[:, cb:cb + 256],
                                                waits=[pbf_free],
                                                stream="vflush")
                                            vstg_free = dma(
                                                "G", vl[trow:trow + 128,
                                                        dvc:dvc + 256],
                                                vstg_l[:, cb:cb + 256],
                                                stream="vflush")
                        st = si * 2 + tb
                        construct_step((16 * cq) // 18 +
                                       (1 if st < (16 * cq) % 18 else 0))
                    wb_free[par] = mm_tok[len(slab) - 1]
            bd.barrier()
            bd.flush()

    expand_phase(xb, NSUP, False, CQ1)

    with contextlib.ExitStack() as qst:
        q_hi = qst.enter_context(nc.sbuf_tensor("q_hi", [128, 2 * OWN], BF16))
        q_lo = qst.enter_context(nc.sbuf_tensor("q_lo", [128, 2 * OWN], BF16))

        expand_phase(xo, NSUP_OWN, True, CQ2)

        with contextlib.ExitStack() as ph:
            psb = lambda n, s, d: ph.enter_context(
                nc.sbuf_tensor(bd.uid(n), s, d))
            psc = ph.enter_context(nc.psum_tensor("psc", [128, 1024], F32))
            pav = ph.enter_context(nc.psum_tensor("pav", [128, 2048], F32))
            ppt = [ph.enter_context(nc.psum_tensor("ppt%d" % i,
                                                   [128, 1024], BF16))
                   for i in range(2)]
            HS = NSLOT // 2
            acc = [psb("acc%d" % i, [128, 2048], F32) for i in range(HS)]
            vbuf_h = [psb("vbh%d" % i, [128, 4 * 2048], BF16) for i in range(2)]
            vbuf_l = [psb("vbl%d" % i, [128, 4 * 2048], BF16) for i in range(2)]
            kbuf_h = [psb("kbh%d" % i, [128, 2 * 512], BF16) for i in range(2)]
            kbuf_l = [psb("kbl%d" % i, [128, 2 * 512], BF16) for i in range(2)]
            s2 = [psb("s2%d" % i, [128, 512], F32) for i in range(2)]
            ppf = [psb("ppf%d" % i, [128, 512], F32) for i in range(2)]
            p_hi = [psb("p_hi%d" % i, [128, 512], BF16) for i in range(2)]
            p_lo = [psb("p_lo%d" % i, [128, 512], BF16) for i in range(2)]
            pt_hi = [psb("pt_hi%d" % i, [128, 512], BF16) for i in range(2)]
            pt_lo = [psb("pt_lo%d" % i, [128, 512], BF16) for i in range(2)]
            fst = psb("fst", [128, 6 * HS], F32)
            atm = psb("atm", [128, 1024], F32)
            a_hi = psb("a_hi", [128, 1024], BF16)
            a_lo = psb("a_lo", [128, 1024], BF16)
            astg = psb("astg", [128, 512], BF16)

            vld_tok = [None, None]
            fin_ppt = [0]
            fin_vt = [None]
            pav_free = None
            ppt_free = [None, None]
            psc_free = [None, None]
            astg_free = None
            s2_free = [None, None]
            af_free = None
            vcnt = [0]
            vprev = [None]
            pavf = [None, None]
            fsum_t = [None]

            def visit_back_a(ctx):
                sbk = ctx['sbk']
                vtp = ctx['vtp']
                phv, plv = ctx['phv'], ctx['plv']
                pthv, ptlv = ctx['pthv'], ctx['ptlv']
                pptv = ctx['pptv']
                tt = None
                for half, srct in ((0, phv), (1, plv)):
                    for ks in range(4):
                        sl = 4 * half + ks
                        tt = op("T", lambda e, srct=srct, ks=ks,
                                sl=sl, pptv=pptv: e.matmul(
                                    pptv[:, 128 * sl:128 * (sl + 1)],
                                    srct[:, 128 * ks:128 * (ks + 1)],
                                    c_idb[:], is_transpose=True,
                                    start=(sl == 0), stop=(sl == 7)),
                                waits=[vtp, ppt_free[sbk]]
                                if sl == 0 else [], inc=(sl == 7))
                op("A", lambda e, pthv=pthv, pptv=pptv:
                   e.copy(pthv[:], pptv[:, 0:512]), waits=[tt])
                ppt_free[sbk] = op(
                    "A", lambda e, ptlv=ptlv, pptv=pptv: e.copy(
                        ptlv[:], pptv[:, 512:1024]), inc=True)

            def visit_back_b(ctx):
                sbk = ctx['sbk']
                mi = ctx['mi']
                Fc = ctx['Fc']
                first = ctx['first']
                at = ctx['at']
                pthv, ptlv = ctx['pthv'], ctx['ptlv']
                for hf in range(2):
                    avtok = None
                    for dv4 in (2 * hf, 2 * hf + 1):
                        pA = pav[:, 512 * dv4:512 * (dv4 + 1)]
                        fmm = True
                        for ks in range(4):
                            vh_s = vbuf_h[ctx['b']][:, 2048 * ks + 512 * dv4:
                                             2048 * ks + 512 * (dv4 + 1)]
                            vl_s = vbuf_l[ctx['b']][:, 2048 * ks + 512 * dv4:
                                             2048 * ks + 512 * (dv4 + 1)]
                            ph_s = pthv[:, 128 * ks:128 * (ks + 1)]
                            pl_s = ptlv[:, 128 * ks:128 * (ks + 1)]
                            w0 = ([ppt_free[sbk], pavf[hf]]
                                  if (fmm and dv4 == 2 * hf) else [])
                            op("T", lambda e, pA=pA, ph_s=ph_s, vh_s=vh_s,
                               f=fmm: e.matmul(pA, ph_s, vh_s, start=f,
                                               stop=False), waits=w0)
                            fmm = False
                            op("T", lambda e, pA=pA, ph_s=ph_s, vl_s=vl_s:
                               e.matmul(pA, ph_s, vl_s, start=False,
                                        stop=False))
                            avtok = op("T", lambda e, pA=pA, pl_s=pl_s,
                                       vh_s=vh_s, l=(ks == 3): e.matmul(
                                           pA, pl_s, vh_s, start=False,
                                           stop=l),
                                       inc=(ks == 3 and dv4 == 2 * hf + 1))
                    hs0 = 1024 * hf
                    if first:
                        pavf[hf] = op("V", lambda e, mi=mi, hs0=hs0:
                                      e.tensor_copy(
                                          acc[mi][:, hs0:hs0 + 1024],
                                          pav[:, hs0:hs0 + 1024]),
                                      waits=[avtok, at], inc=True)
                    else:
                        pavf[hf] = op("V", lambda e, mi=mi, Fc=Fc, hs0=hs0:
                                      e.scalar_tensor_tensor(
                                          acc[mi][:, hs0:hs0 + 1024],
                                          acc[mi][:, hs0:hs0 + 1024],
                                          Fc[:, 5:6],
                                          pav[:, hs0:hs0 + 1024],
                                          ALU.mult, ALU.add),
                                      waits=[avtok, at], inc=True)
                if first:
                    fsum = op("V", lambda e, Fc=Fc: e.tensor_copy(
                        Fc[:, 2:3], Fc[:, 3:4]), inc=True)
                else:
                    fsum = op("V", lambda e, Fc=Fc:
                              e.scalar_tensor_tensor(
                                  Fc[:, 2:3], Fc[:, 2:3],
                                  Fc[:, 5:6], Fc[:, 3:4],
                                  ALU.mult, ALU.add), inc=True)
                fsum_t[0] = fsum
                if ctx['last']:
                    vld_tok[ctx['b']] = pavf[1]

            for sweep in range(2):
                m_lo, m_hi = sweep * HS, (sweep + 1) * HS
                for c in range(m_hi):
                    b = c % 2
                    lt = None
                    for ks in range(4):
                        r0 = 512 * c + 128 * ks
                        lt = dma("S", vbuf_h[b][:, 2048 * ks:2048 * (ks + 1)],
                                 vh[r0:r0 + 128, :], waits=[vld_tok[b]],
                                 stream="vk%d" % b)
                        lt = dma("S", vbuf_l[b][:, 2048 * ks:2048 * (ks + 1)],
                                 vl[r0:r0 + 128, :], stream="vk%d" % b)
                    for qc in range(2):
                        lt = dma("S", kbuf_h[b][:, 512 * qc:512 * (qc + 1)],
                                 kh[128 * qc:128 * (qc + 1),
                                    512 * c:512 * (c + 1)], stream="vk%d" % b)
                        lt = dma("S", kbuf_l[b][:, 512 * qc:512 * (qc + 1)],
                                 kl[128 * qc:128 * (qc + 1),
                                    512 * c:512 * (c + 1)], stream="vk%d" % b)
                    for m in range(max(c, m_lo), m_hi):
                        mi = m - m_lo
                        if vprev[0] is not None:
                            visit_back_a(vprev[0])
                        if vprev[0] is not None and vprev[0]['mi'] == mi:
                            visit_back_b(vprev[0])
                            vprev[0] = None
                        first = (c == 0)
                        diag = (c == m)
                        sbk = vcnt[0] % 2
                        vcnt[0] += 1
                        pS = psc[:, 512 * sbk:512 * (sbk + 1)]
                        s2v = s2[sbk]
                        ppfv = ppf[sbk]
                        phv, plv = p_hi[sbk], p_lo[sbk]
                        pthv, ptlv = pt_hi[sbk], pt_lo[sbk]
                        pptv = ppt[sbk]
                        so = 4 * sbk
                        stok = None
                        for qc in range(2):
                            qh_s = q_hi[:, qc * OWN + 128 * m:
                                        qc * OWN + 128 * (m + 1)]
                            ql_s = q_lo[:, qc * OWN + 128 * m:
                                        qc * OWN + 128 * (m + 1)]
                            kh_s = kbuf_h[b][:, 512 * qc:512 * (qc + 1)]
                            kl_s = kbuf_l[b][:, 512 * qc:512 * (qc + 1)]
                            op("T", lambda e, pS=pS, qh_s=qh_s, kh_s=kh_s,
                               f=(qc == 0): e.matmul(pS, qh_s, kh_s, start=f,
                                                     stop=False),
                               waits=[lt, psc_free[sbk]] if qc == 0 else [])
                            op("T", lambda e, pS=pS, qh_s=qh_s, kl_s=kl_s:
                               e.matmul(pS, qh_s, kl_s, start=False,
                                        stop=False))
                            stok = op("T", lambda e, pS=pS, ql_s=ql_s,
                                      kh_s=kh_s, l=(qc == 1): e.matmul(
                                          pS, ql_s, kh_s, start=False, stop=l),
                                      inc=(qc == 1))
                        mask = tdiag if diag else mR
                        psc_free[sbk] = op(
                            "V", lambda e, pS=pS, mask=mask, s2v=s2v:
                            e.scalar_tensor_tensor(s2v[:], pS, SCALE_QK,
                                                   mask[:], ALU.mult, ALU.add),
                            waits=[stok, s2_free[sbk]], sync_after=True)
                        op("V", lambda e, s2v=s2v, so=so: e.tensor_reduce(
                            smt[:, so:so + 1], s2v[:], AX.X, ALU.max,
                            negate=True), sync_after=True)
                        if not diag:
                            kk = m - c
                            op("V", lambda e, kk=kk, so=so: e.tensor_tensor(
                                smt[:, so:so + 1], smt[:, so:so + 1],
                                mdt[:, kk:kk + 1], ALU.subtract),
                               sync_after=True)
                        Fc = fst[:, 6 * mi:6 * mi + 6]
                        if first:
                            op("V", lambda e, Fc=Fc, so=so: e.tensor_copy(
                                Fc[:, 0:1], smt[:, so:so + 1]),
                               sync_after=True)
                        else:
                            op("V", lambda e, Fc=Fc: e.tensor_copy(
                                Fc[:, 1:2], Fc[:, 0:1]), sync_after=True)
                            op("V", lambda e, Fc=Fc, so=so: e.tensor_tensor(
                                Fc[:, 0:1], Fc[:, 1:2], smt[:, so:so + 1],
                                ALU.min), sync_after=True)
                            op("V", lambda e, Fc=Fc: e.tensor_tensor(
                                Fc[:, 4:5], Fc[:, 0:1], Fc[:, 1:2],
                                ALU.subtract), sync_after=True)
                        if diag:
                            vt = op("V", lambda e, Fc=Fc, so=so: e.tensor_copy(
                                smt[:, so + 1:so + 2], Fc[:, 0:1]), inc=True)
                        else:
                            kk = m - c
                            vt = op("V", lambda e, Fc=Fc, kk=kk, so=so:
                                    e.tensor_tensor(smt[:, so + 1:so + 2],
                                                    Fc[:, 0:1],
                                                    mdt[:, kk:kk + 1],
                                                    ALU.add), inc=True)
                        s2_free[sbk] = op(
                            "A", lambda e, Fc=Fc, ppfv=ppfv, s2v=s2v, so=so:
                            e.activation(
                                ppfv[:], s2v[:], AF.Exp,
                                bias=smt[:, so + 1:so + 2],
                                accum_out=Fc[:, 3:4]), waits=[vt],
                            sync_after=True)
                        at = s2_free[sbk]
                        if not first:
                            at = op("A", lambda e, Fc=Fc: e.activation(
                                Fc[:, 5:6], Fc[:, 4:5], AF.Exp),
                                sync_after=True)
                        at_p = op("A", lambda e, phv=phv, ppfv=ppfv:
                                  e.copy(phv[:], ppfv[:]),
                                  waits=[at], sync_after=True)
                        vtp = op("V",
                                 lambda e, plv=plv, ppfv=ppfv, phv=phv:
                                 e.tensor_tensor(
                                     plv[:], ppfv[:], phv[:], ALU.subtract),
                                 waits=[at_p], inc=True)
                        ctx = dict(sbk=sbk, b=b, mi=mi, Fc=Fc,
                                   first=first, at=at, vtp=vtp,
                                   phv=phv, plv=plv, pthv=pthv,
                                   ptlv=ptlv, pptv=pptv,
                                   last=(m == m_hi - 1))
                        if vprev[0] is not None:
                            visit_back_b(vprev[0])
                        vprev[0] = ctx
                        construct_step(5)


                if vprev[0] is not None:
                    visit_back_a(vprev[0])
                    visit_back_b(vprev[0])
                    vprev[0] = None
                for mi in range(HS):
                    m = m_lo + mi
                    Fc = fst[:, 6 * mi:6 * mi + 6]
                    op("V", lambda e, Fc=Fc: e.reciprocal(
                        smt[:, 2:3], Fc[:, 2:3]),
                       waits=[pavf[0], pavf[1], fsum_t[0]],
                       sync_after=True)
                    for ch in range(2):
                        vt_n = op("V", lambda e, mi=mi, ch=ch:
                                  e.tensor_scalar(
                                      atm[:], acc[mi][:, 1024 * ch:
                                                      1024 * (ch + 1)],
                                      smt[:, 2:3], None, ALU.mult),
                                  waits=[fin_vt[0]], sync_after=True)
                        at_a = op("A", lambda e: e.copy(a_hi[:], atm[:]),
                                  waits=[af_free, vt_n], sync_after=True)
                        vt = op("V", lambda e: e.tensor_tensor(
                            a_lo[:], atm[:], a_hi[:], ALU.subtract),
                            waits=[at_a], inc=True)
                        fin_vt[0] = vt
                        for srct, dstt in ((a_hi, afh), (a_lo, afl)):
                            for rl in range(2):
                                rnd = 2 * ch + rl
                                fp = fin_ppt[0] % 2
                                fin_ppt[0] += 1
                                pptv = ppt[fp]
                                tt = None
                                for q4 in range(4):
                                    dc = 4 * rl + q4
                                    tt = op("T", lambda e, srct=srct, dc=dc,
                                            q4=q4, pptv=pptv: e.matmul(
                                                pptv[:, 128 * q4:
                                                     128 * (q4 + 1)],
                                                srct[:, 128 * dc:
                                                     128 * (dc + 1)],
                                                c_idb[:], is_transpose=True,
                                                start=(q4 == 0),
                                                stop=(q4 == 3)),
                                            waits=[vt, ppt_free[fp]]
                                            if q4 == 0 else [], inc=(q4 == 3))
                                af_free = tt
                                ppt_free[fp] = op(
                                    "A", lambda e, pptv=pptv: e.copy(
                                        astg[:], pptv[:, 0:512]),
                                    waits=[tt, astg_free], inc=True)
                                astg_free = dma("G", dstt.rearrange(
                                    "(c p) t -> p c t", p=128)[
                                        :, 4 * rnd:4 * rnd + 4,
                                        128 * m:128 * (m + 1)],
                                    astg.ap().rearrange("p (c t) -> p c t",
                                                        c=4),
                                    waits=[ppt_free[fp]], stream="astg")
            bd.barrier()
            bd.flush()

        NCHK = OWN // 512
        for u in range(NCHK):
            t0 = 512 * u
            with contextlib.ExitStack() as ph:
                psb = lambda n, s, d: ph.enter_context(
                    nc.sbuf_tensor(bd.uid(n), s, d))
                pA = ph.enter_context(
                    nc.psum_tensor(bd.uid("p5a"), [128, 2048], F32))
                pB = ph.enter_context(
                    nc.psum_tensor(bd.uid("p5b"), [128, 2048], F32))
                x2 = psb("x2", [128, DCH * 512], F32)
                h2h = psb("h2h", [128, DCH * 512], BF16)
                h2l = psb("h2l", [128, DCH * 512], BF16)
                lw_h = [psb("lwh%d" % i, [128, 512], BF16) for i in range(2)]
                lw_l = [psb("lwl%d" % i, [128, 512], BF16) for i in range(2)]
                rh_h = [psb("rhh%d" % i, [128, 512], BF16) for i in range(2)]
                rh_l = [psb("rhl%d" % i, [128, 512], BF16) for i in range(2)]
                r0b = psb("r0b", [128, 2048], F32)
                wf_h = [psb("wfh%d" % i, [128, DCH * 256], BF16)
                        for i in range(2)]
                wf_l = [psb("wfl%d" % i, [128, DCH * 256], BF16)
                        for i in range(2)]
                sg = psb("sg", [128, 512], F32)
                hsf = psb("hsf", [128, 512], F32)
                e_hi = psb("e_hi5", [128, 512], BF16)
                e_lo = psb("e_lo5", [128, 512], BF16)
                x2t = psb("x2t", [128, 2048], F32)
                h2t = psb("h2t", [128, 2048], F32)
                sst = psb("sst5", [128, 8], F32)
                ytile = psb("ytile", [128, 2048], F32)

                x2v = x2.ap().rearrange("p (c t) -> p c t", c=DCH)
                h2hv = h2h.ap().rearrange("p (c t) -> p c t", c=DCH)
                h2lv = h2l.ap().rearrange("p (c t) -> p c t", c=DCH)
                outv = x2v

                lw_free = [None, None]
                pA_free = None
                pB_free = None
                y_free = None
                vt5 = None
                for dcg in range(4):
                    pacc = pA if dcg % 2 == 0 else pB
                    pfree = pA_free if dcg % 2 == 0 else pB_free
                    ptok = None
                    for ec in range(32):
                        bb = ec % 2
                        gsh = glh if ec < 16 else afh
                        gsl = gll if ec < 16 else afl
                        er = 128 * (ec % 16)
                        l1 = dma("S", lw_h[bb][:],
                                 pTh[128 * ec:128 * (ec + 1),
                                     512 * dcg:512 * (dcg + 1)],
                                 waits=[lw_free[bb]], stream="lr%d" % bb)
                        l2 = dma("S", lw_l[bb][:],
                                 pTl[128 * ec:128 * (ec + 1),
                                     512 * dcg:512 * (dcg + 1)],
                                 stream="lr%d" % bb)
                        l3 = dma("S", rh_h[bb][:], gsh[er:er + 128,
                                                       t0:t0 + 512],
                                 stream="lr%d" % bb)
                        l4 = dma("S", rh_l[bb][:], gsl[er:er + 128,
                                                       t0:t0 + 512],
                                 stream="lr%d" % bb)
                        for dc4 in range(4):
                            pp5 = pacc[:, 512 * dc4:512 * (dc4 + 1)]
                            lhsh = lw_h[bb][:, 128 * dc4:128 * (dc4 + 1)]
                            lhsl = lw_l[bb][:, 128 * dc4:128 * (dc4 + 1)]
                            w0 = ([l1, l2, l3, l4, pfree]
                                  if dc4 == 0 else [])
                            op("T", lambda e, pp5=pp5, lhsh=lhsh, bb=bb,
                               f=(ec == 0): e.matmul(pp5, lhsh, rh_h[bb][:],
                                                     start=f, stop=False),
                               waits=w0)
                            op("T", lambda e, pp5=pp5, lhsh=lhsh, bb=bb:
                               e.matmul(pp5, lhsh, rh_l[bb][:], start=False,
                                        stop=False))
                            ptok = op("T", lambda e, pp5=pp5, lhsl=lhsl,
                                      bb=bb, l=(ec == 31): e.matmul(
                                          pp5, lhsl, rh_h[bb][:], start=False,
                                          stop=l),
                                      inc=(dc4 == 3))
                        lw_free[bb] = ptok
                    ldr = dma("S",
                              r0b.ap().rearrange("p (c t) -> p c t", c=4),
                              r0s3[:, 4 * dcg:4 * dcg + 4, t0:t0 + 512],
                              waits=[vt5], stream="r0b")
                    vt5 = None
                    for dc4 in range(4):
                        dc = 4 * dcg + dc4
                        vt5 = op("V", lambda e, pacc=pacc, dc4=dc4, dc=dc:
                                 e.scalar_tensor_tensor(
                                     x2v[:, dc, :],
                                     pacc[:, 512 * dc4:512 * (dc4 + 1)], 1.0,
                                     r0b[:, 512 * dc4:512 * (dc4 + 1)],
                                     ALU.mult, ALU.add),
                                 waits=[ptok, ldr], inc=(dc4 == 3))
                    if dcg % 2 == 0:
                        pA_free = vt5
                    else:
                        pB_free = vt5
                x2tok = vt5

                h2tok = None
                for psub in range(4):
                    tt = None
                    for rnd in range(2):
                        for cc in range(8):
                            c = 8 * rnd + cc
                            sl = 8 * rnd + cc
                            tt = op("T", lambda e, c=c, sl=sl, psub=psub:
                                    e.matmul(
                                        pA[:, 128 * sl:128 * (sl + 1)],
                                        x2v[:, c, 128 * psub:128 * (psub + 1)],
                                        c_idf[:], is_transpose=True,
                                        start=(sl % 4 == 0),
                                        stop=(sl % 4 == 3)),
                                    waits=[x2tok, pA_free]
                                    if sl == 0 else [], inc=(sl == 15))
                    pA_free = op("A", lambda e: e.copy(x2t[:], pA[:]),
                                 waits=[tt], sync_after=True)
                    vt = ln_tm(x2t, h2t, sst, extra_wait=[pA_free])
                    tt = None
                    for rnd in range(2):
                        for cc in range(8):
                            c = 8 * rnd + cc
                            sl = 8 * rnd + cc
                            tt = op("T", lambda e, c=c, sl=sl: e.matmul(
                                pB[:, 128 * sl:128 * (sl + 1)],
                                h2t[:, 128 * c:128 * (c + 1)], c_idf[:],
                                is_transpose=True, start=(sl % 4 == 0),
                                stop=(sl % 4 == 3)),
                                waits=[vt, pB_free] if sl == 0 else [],
                                inc=(sl == 15))
                    pBv = pB.ap().rearrange("p (c t) -> p c t", c=DCH)
                    at_h2 = op("A", lambda e, psub=psub: e.copy(
                        h2hv[:, :, 128 * psub:128 * (psub + 1)], pBv),
                        waits=[tt], sync_after=True)
                    h2tok = op("V", lambda e, psub=psub: e.tensor_tensor(
                        h2lv[:, :, 128 * psub:128 * (psub + 1)], pBv,
                        h2hv[:, :, 128 * psub:128 * (psub + 1)],
                        ALU.subtract), waits=[at_h2], inc=True)
                    pB_free = h2tok

                wf_free = [None, None]
                hs_free = None
                sg_free = None
                for og in range(22):
                    construct_step(32 if og < 16 else 16)
                    ocs = [i for i in (2 * og, 2 * og + 1) if i < ICH]
                    noc = len(ocs)
                    toks = {}
                    for mat, (mh, ml) in ((0, (wgh, wgl)), (1, (wuh, wul))):
                        par = (2 * og + mat) % 2
                        lt = None
                        for dc in range(DCH):
                            lt = dma("S", wf_h[par][:, dc * 256:
                                                    dc * 256 + 128 * noc],
                                     mh[128 * dc:128 * (dc + 1),
                                        256 * og:256 * og + 128 * noc],
                                     waits=[wf_free[par]],
                                     stream="wf%d" % par)
                            lt = dma("S", wf_l[par][:, dc * 256:
                                                    dc * 256 + 128 * noc],
                                     ml[128 * dc:128 * (dc + 1),
                                        256 * og:256 * og + 128 * noc],
                                     stream="wf%d" % par)
                        pacc = pA if mat == 0 else pB
                        pfree = pA_free if mat == 0 else pB_free
                        mmt = None
                        for oi, oc in enumerate(ocs):
                            po = pacc[:, 512 * oi:512 * (oi + 1)]
                            first = True
                            for dc in range(DCH):
                                lh = wf_h[par][:, (dc * 2 + oi) * 128:
                                               (dc * 2 + oi + 1) * 128]
                                llo = wf_l[par][:, (dc * 2 + oi) * 128:
                                                (dc * 2 + oi + 1) * 128]
                                rhh = h2hv[:, dc, :]
                                rhl = h2lv[:, dc, :]
                                w0 = ([lt, h2tok, pfree] if first else [])
                                op("T", lambda e, po=po, lh=lh, rhh=rhh,
                                   f=first: e.matmul(po, lh, rhh, start=f,
                                                     stop=False), waits=w0)
                                first = False
                                op("T", lambda e, po=po, lh=lh, rhl=rhl:
                                   e.matmul(po, lh, rhl, start=False,
                                            stop=False))
                                mmt = op("T", lambda e, po=po, llo=llo,
                                         rhh=rhh, l=(dc == DCH - 1):
                                         e.matmul(po, llo, rhh, start=False,
                                                  stop=l),
                                         inc=(dc == DCH - 1))
                        toks[mat] = mmt
                        wf_free[par] = mmt
                    gate_tok, up_tok = toks[0], toks[1]
                    vt6 = None
                    for oi, oc in enumerate(ocs):
                        at = op("A", lambda e, oi=oi: e.activation(
                            sg[:], pA[:, 512 * oi:512 * (oi + 1)], AF.Silu),
                            waits=[gate_tok, sg_free], inc=True)
                        sg_free = op("V", lambda e, oi=oi: e.tensor_tensor(
                            hsf[:], pB[:, 512 * oi:512 * (oi + 1)], sg[:],
                            ALU.mult), waits=[at, up_tok], sync_after=True)
                        at_e5 = op("A", lambda e: e.copy(e_hi[:], hsf[:]),
                                   waits=[hs_free, sg_free], sync_after=True)
                        vt6 = op("V", lambda e: e.tensor_tensor(
                            e_lo[:], hsf[:], e_hi[:], ALU.subtract),
                            waits=[at_e5], sync_after=True)
                        dma("G", hsh[128 * oc:128 * (oc + 1),
                                     t0:t0 + 512], e_hi[:], waits=[vt6],
                            stream="hsout")
                        hs_free = dma("G", hsl[128 * oc:128 * (oc + 1),
                                              t0:t0 + 512], e_lo[:],
                                      stream="hsout")
                    pA_free = vt6
                    pB_free = vt6

                dwn_tok = cstate["busy"] if u == 0 else None
                out_tok = None
                for dcg in range(4):
                    pacc = pA if dcg % 2 == 0 else pB
                    pfree = pA_free if dcg % 2 == 0 else pB_free
                    dtok = None
                    for ic in range(ICH):
                        bb = ic % 2
                        l1 = dma("S", lw_h[bb][:],
                                 wdh[128 * ic:128 * (ic + 1),
                                     512 * dcg:512 * (dcg + 1)],
                                 waits=[lw_free[bb], dwn_tok],
                                 stream="lr%d" % bb)
                        l2 = dma("S", lw_l[bb][:],
                                 wdl[128 * ic:128 * (ic + 1),
                                     512 * dcg:512 * (dcg + 1)],
                                 stream="lr%d" % bb)
                        l3 = dma("S", rh_h[bb][:],
                                 hsh[128 * ic:128 * (ic + 1), t0:t0 + 512],
                                 waits=[hs_free], stream="lr%d" % bb)
                        l4 = dma("S", rh_l[bb][:],
                                 hsl[128 * ic:128 * (ic + 1), t0:t0 + 512],
                                 stream="lr%d" % bb)
                        for dc4 in range(4):
                            pp5 = pacc[:, 512 * dc4:512 * (dc4 + 1)]
                            lhsh = lw_h[bb][:, 128 * dc4:128 * (dc4 + 1)]
                            lhsl = lw_l[bb][:, 128 * dc4:128 * (dc4 + 1)]
                            w0 = ([l1, l2, l3, l4, pfree] if dc4 == 0 else [])
                            op("T", lambda e, pp5=pp5, lhsh=lhsh, bb=bb,
                               f=(ic == 0): e.matmul(pp5, lhsh, rh_h[bb][:],
                                                     start=f, stop=False),
                               waits=w0)
                            op("T", lambda e, pp5=pp5, lhsh=lhsh, bb=bb:
                               e.matmul(pp5, lhsh, rh_l[bb][:], start=False,
                                        stop=False))
                            dtok = op("T", lambda e, pp5=pp5, lhsl=lhsl,
                                      bb=bb, l=(ic == ICH - 1): e.matmul(
                                          pp5, lhsl, rh_h[bb][:],
                                          start=False, stop=l),
                                      inc=(dc4 == 3))
                        lw_free[bb] = dtok
                    for dc4 in range(4):
                        dc = 4 * dcg + dc4
                        out_tok = op("V", lambda e, pacc=pacc, dc4=dc4, dc=dc:
                                     e.scalar_tensor_tensor(
                                         outv[:, dc, :],
                                         pacc[:, 512 * dc4:512 * (dc4 + 1)],
                                         1.0, x2v[:, dc, :], ALU.mult,
                                         ALU.add), waits=[dtok],
                                     inc=(dc4 == 3))
                    if dcg % 2 == 0:
                        pA_free = out_tok
                    else:
                        pB_free = out_tok

                for psub in range(4):
                    tt = None
                    for rnd in range(2):
                        for cc in range(8):
                            c = 8 * rnd + cc
                            sl = 8 * rnd + cc
                            tt = op("T", lambda e, c=c, sl=sl, psub=psub:
                                    e.matmul(
                                        pA[:, 128 * sl:128 * (sl + 1)],
                                        outv[:, c,
                                             128 * psub:128 * (psub + 1)],
                                        c_idf[:], is_transpose=True,
                                        start=(sl % 4 == 0),
                                        stop=(sl % 4 == 3)),
                                    waits=[out_tok, pA_free]
                                    if sl == 0 else [], inc=(sl == 15))
                    pA_free = op("A", lambda e: e.copy(
                        ytile[:], pA[:]), waits=[tt, y_free], inc=True)
                    y_free = dma("G", y[t0 + 128 * psub:
                                        t0 + 128 * (psub + 1), :],
                                 ytile[:], waits=[pA_free], stream="yout")
                bd.barrier()
                bd.flush()

    assert cstate["i"] >= len(cjobs), (cstate["i"], len(cjobs))
    bd.st.close()
    return nc


_CACHE = {}


def _get_nc(SB):
    if SB not in _CACHE:
        _CACHE[SB] = build_nc(SB)
    return _CACHE[SB]


def make_in_maps(SB, x_full, expand, project, gidx, uidx, didx, lut, pm_s,
                 gamma, beta):
    OWN = SB // 4
    eT = np.ascontiguousarray(expand.T)
    pT = np.ascontiguousarray(project.T)
    gT = np.ascontiguousarray(gidx.T).astype(ml_dtypes.bfloat16)
    uT = np.ascontiguousarray(uidx.T).astype(ml_dtypes.bfloat16)
    dT = np.ascontiguousarray(didx.T).astype(ml_dtypes.bfloat16)
    lut16 = np.ascontiguousarray(lut.reshape(1, 16))
    posm = np.ascontiguousarray(np.asarray(pm_s, np.float32).reshape(1, 1))
    gam = np.ascontiguousarray(gamma.reshape(DCH, 128).T)
    bet = np.ascontiguousarray(beta.reshape(DCH, 128).T)
    ones_ = np.ones((1, 128), np.float32)
    idf = np.eye(128, dtype=np.float32)
    idb = np.eye(128, dtype=BF)
    fi = np.arange(512, dtype=np.float32)
    pi = np.arange(128, dtype=np.float32)
    ramp = fi[None, :] - pi[:, None]
    csc = np.zeros((128, 4), np.float32)
    csc[:, 0] = EPS
    csc[:, 1] = 1.0
    jneg = np.tile(np.arange(16, dtype=np.float32), (128, 1))

    in_maps = []
    for c in range(8):
        bidx, r = divmod(c, 4)
        xbat = x_full[bidx]
        own_rows = np.concatenate(
            [np.arange(512 * m + 128 * r, 512 * m + 128 * r + 128)
             for m in range(SB // 512)])
        xo = np.ascontiguousarray(xbat[own_rows])
        keepm = (fi[None, :] <= 128 * r + pi[:, None]).astype(np.float32)
        dtab = np.zeros((128, 16), np.float32)
        for k in range(16):
            dtab[:, k] = -512.0 * k - 128.0 * r
        in_maps.append({
            "xb": np.ascontiguousarray(xbat),
            "xo": xo,
            "xoT": np.ascontiguousarray(xo.T),
            "eT": eT, "pT": pT, "gidxT": gT, "uidxT": uT, "didxT": dT,
            "lut16": lut16, "posm": posm, "gam": gam, "bet": bet,
            "ones1": ones_, "idf": idf, "idb": idb,
            "ramp": np.ascontiguousarray(ramp),
            "keep": keepm, "dtab": dtab, "csc": csc, "jneg": jneg,
        })
    return in_maps, [
        np.concatenate([np.arange(512 * m + 128 * (c % 4),
                                  512 * m + 128 * (c % 4) + 128)
                        for m in range(SB // 512)]) for c in range(8)]


def kernel(L0_attn_x, L0_attn_ln_gamma, L0_attn_ln_beta, L0_attn_expand,
           L0_attn_project, L0_attn_pos_mult, L0_ffn_lut, L0_ffn_gate_idx,
           L0_ffn_up_idx, L0_ffn_down_idx):
    SB = L0_attn_x.shape[1]
    x = np.ascontiguousarray(np.asarray(L0_attn_x, np.float32))
    nc = _get_nc(SB)
    in_maps, rows = make_in_maps(
        SB, x, np.asarray(L0_attn_expand, np.float32),
        np.asarray(L0_attn_project, np.float32),
        np.asarray(L0_ffn_gate_idx, np.int32),
        np.asarray(L0_ffn_up_idx, np.int32),
        np.asarray(L0_ffn_down_idx, np.int32),
        np.asarray(L0_ffn_lut, np.float32), L0_attn_pos_mult,
        np.asarray(L0_attn_ln_gamma, np.float32),
        np.asarray(L0_attn_ln_beta, np.float32))
    res = run_bass_kernel_spmd(nc, in_maps, core_ids=list(range(8)))
    out = np.empty((B, SB, D), np.float32)
    for c in range(8):
        out[c // 4][rows[c]] = res.results[c]["y"]
    return out
